# version 08
# baseline (speedup 1.0000x reference)
import os
import numpy as np
from contextlib import ExitStack

KCUT = int(os.environ.get("KCUT", "-1"))
KTAPS = [t for t in os.environ.get("KTAPS", "").split(",") if t]
KNOLN = os.environ.get("KNOLN", "") == "1"

import concourse.bass as bass
import concourse.mybir as mybir
from concourse.bass_utils import run_bass_kernel_spmd

F32 = mybir.dt.float32
BF16 = mybir.dt.bfloat16
AF = mybir.ActivationFunctionType
ALU = mybir.AluOpType
AX = mybir.AxisListType

P = 128
T = 2048
NT = 16
D = 1024
ND = 8
H = 8
DH = 64
INNER = 512
NW = 4
HID = 2730
NHT = 22
HALVES = [list(range(0, 11)), list(range(11, 22))]
NEG = -1.0e9
PAIRS = [[0, 1], [2, 3], [4, 5], [6, 7]]


class Ch:
    def __init__(self, nc, es, name):
        self.sem = es.enter_context(nc.semaphore(name))
        self.n = 0

    def inc(self, k=1):
        self.n += k
        return (self.sem, k)

    def at(self):
        return (self.sem, self.n)


class KB:
    def __init__(self, nc, es):
        self.nc = nc
        self.ops = {"t": [], "v": [], "s": [], "g": [], "y": []}
        self.dy = Ch(nc, es, "dy")
        self.dg = Ch(nc, es, "dg")
        self.tt = Ch(nc, es, "tt")
        self.vv = Ch(nc, es, "vv")
        self.ss = Ch(nc, es, "ss")
        self.gg = Ch(nc, es, "gg")
        self.cc = Ch(nc, es, "cc")
        self.bar = es.enter_context(nc.semaphore("bar"))
        self.barn = 0
        self.cuts = []

    def op(self, e, f, waits=(), incs=()):
        waits = tuple(w for w in waits if w is not None)
        incs = tuple(incs)

        def run(eng, f=f, waits=waits, incs=incs):
            for sem, v in waits:
                eng.wait_ge(sem, v)
            inst = f(eng)
            for sem, d in incs:
                inst = inst.then_inc(sem, d)
        self.ops[e].append(run)

    def t(self, f, waits=(), mark=False):
        self.op("t", f, waits, incs=[self.tt.inc()] if mark else ())
        return self.tt.at()

    def v(self, f, waits=()):
        self.op("v", f, waits, incs=[self.vv.inc()])
        return self.vv.at()

    def s(self, f, waits=()):
        self.op("s", f, waits, incs=[self.ss.inc()])
        return self.ss.at()

    def g(self, f, waits=()):
        self.op("g", f, waits, incs=[self.gg.inc()])
        return self.gg.at()

    def ldy(self, f, waits=()):
        pre = self.dy.at()
        self.op("y", f, tuple(waits) + (pre,), incs=[self.dy.inc(16)])
        return self.dy.at()

    def ldg(self, f, waits=()):
        pre = self.dg.at()
        self.op("g", f, tuple(waits) + (pre,), incs=[self.dg.inc(16)])
        return self.dg.at()

    def tap_raw(self, nc, name, ap_fn, shape, dtype, waits):
        if name not in KTAPS:
            return
        h = nc.dram_tensor("tap_" + name, shape, dtype, kind="ExternalOutput")
        self.ldy(lambda y: y.dma_start(out=h.ap(), in_=ap_fn()), waits=waits)

    def tap_xres(self, nc, name, x_res, waits):
        if name not in KTAPS:
            return
        h = nc.dram_tensor("tap_" + name, [T, D], F32, kind="ExternalOutput")
        self.ldy(lambda y: y.dma_start(
            out=h.ap().rearrange("(a p) d -> p a d", p=P),
            in_=x_res[:, :, :]), waits=waits)

    def barrier(self):
        self.barn += 5
        tgt = self.barn
        for e in "tvsgy":
            def run(eng, tgt=tgt):
                eng.sem_inc(self.bar, 1)
                eng.wait_ge(self.bar, tgt)
            self.ops[e].append(run)
        self.cuts.append({e: len(self.ops[e]) for e in "tvsgy"})


def build_nc():
    nc = bass.Bass("TRN2", target_bir_lowering=False)
    nc.detect_race_conditions = False
    est = ExitStack()
    est.__enter__()
    E = est.enter_context

    def din(name, shape, dtype=F32):
        return nc.dram_tensor(name, shape, dtype, kind="ExternalInput")

    x_in = din("x_in", [T, D])
    wdT = din("wdT", [D, D]); wsT = din("wsT", [D, D]); wiT = din("wiT", [D, D])
    wgT = din("wgT", [D, D]); woT = din("woT", [D, D])
    la = din("la", [P, ND])
    wqkv = [din(f"wqkv{l}", [D, 1536]) for l in range(2)]
    wo_w = [din(f"wo{l}", [INNER, D]) for l in range(2)]
    w1_w = [din(f"w1_{l}", [D, 2 * HID]) for l in range(2)]
    w2_w = [din(f"w2_{l}", [HID, D]) for l in range(2)]
    qs_w = [din(f"qs{l}", [P, DH]) for l in range(2)]
    ks_w = [din(f"ks{l}", [P, DH]) for l in range(2)]
    lng_w = [din(f"lng{l}", [P, ND]) for l in range(2)]
    lnb_w = [din(f"lnb{l}", [P, ND]) for l in range(2)]
    b1a_w = [din(f"b1a{l}", [P, NHT]) for l in range(2)]
    b1g_w = [din(f"b1g{l}", [P, NHT]) for l in range(2)]
    b2_w = [din(f"b2_{l}", [1, D]) for l in range(2)]
    csq = din("csq", [T, DH]); ssq = din("ssq", [T, DH])
    csk = din("csk", [T, DH]); ssk = din("ssk", [T, DH])
    identF = din("identF", [P, P]); identB = din("identB", [P, P], BF16)
    onesB = din("onesB", [P, P], BF16); onesF = din("onesF", [P, P])
    triNL = din("triNL", [P, P]); triNU = din("triNU", [P, P])
    mask_std = din("mask_std", [P, 640])
    mask_w0 = din("mask_w0", [4, P, 640])
    cmask = din("cmask", [P, 1])
    hbias = din("hbias", [P, 1])

    y_out = nc.dram_tensor("y_out", [T, D], F32, kind="ExternalOutput")

    x1_d = nc.dram_tensor("x1_d", [T, D], F32)
    gP_d = nc.dram_tensor("gP_d", [ND, P, T], BF16)
    ag1i = nc.dram_tensor("ag1i", [ND, P], F32)
    ag1o = nc.dram_tensor("ag1o", [2 * ND, P], F32)
    ag2i = [nc.dram_tensor(f"ag2i{l}", [1, D], F32) for l in range(2)]
    ag2o = [nc.dram_tensor(f"ag2o{l}", [2, D], F32) for l in range(2)]
    ag4i = [nc.dram_tensor(f"ag4i{l}", [1, D], F32) for l in range(2)]
    ag4o = [nc.dram_tensor(f"ag4o{l}", [2, D], F32) for l in range(2)]
    kvi = [nc.dram_tensor(f"kvi{l}", [P, 4096], BF16) for l in range(2)]
    kvo = [nc.dram_tensor(f"kvo{l}", [2 * P, 4096], BF16) for l in range(2)]

    kb = KB(nc, est)

    x_res = E(nc.sbuf_tensor("x_res", [P, NT, D], F32))
    idF = E(nc.sbuf_tensor("idF", [P, P], F32))
    idB = E(nc.sbuf_tensor("idB", [P, P], BF16))
    onB = E(nc.sbuf_tensor("onB", [P, P], BF16))
    onF = E(nc.sbuf_tensor("onF", [P, P], F32))
    trL = E(nc.sbuf_tensor("trL", [P, P], F32))
    trU = E(nc.sbuf_tensor("trU", [P, P], F32))
    a_col = E(nc.sbuf_tensor("a_col", [P, ND], F32))
    carry = E(nc.sbuf_tensor("carry", [P, ND], F32))
    fin = E(nc.sbuf_tensor("fin", [P, ND], F32))
    cm_col = E(nc.sbuf_tensor("cm_col", [P, 1], F32))
    hb_col = E(nc.sbuf_tensor("hb_col", [P, 1], F32))
    la_sb = E(nc.sbuf_tensor("la_sb", [P, ND], F32))
    lng_c = E(nc.sbuf_tensor("lng_c", [P, ND], F32))
    lnb_c = E(nc.sbuf_tensor("lnb_c", [P, ND], F32))
    b1a_c = E(nc.sbuf_tensor("b1a_c", [P, NHT], F32))
    b1g_c = E(nc.sbuf_tensor("b1g_c", [P, NHT], F32))
    qs_b = E(nc.sbuf_tensor("qs_b", [P, DH], BF16))
    ks_b = E(nc.sbuf_tensor("ks_b", [P, DH], BF16))
    den_s = E(nc.sbuf_tensor("den_s", [1, INNER], F32))
    rd_s = E(nc.sbuf_tensor("rd_s", [1, INNER], F32))
    nmrow = E(nc.sbuf_tensor("nmrow", [1, INNER], BF16))
    cols = E(nc.sbuf_tensor("cols", [P, 32], F32))
    cst = E(nc.sbuf_tensor("cst", [P, 4], F32))
    fA = [E(nc.sbuf_tensor(f"fA{i}", [P, D], F32)) for i in range(6)]
    bA = [E(nc.sbuf_tensor(f"bA{i}", [P, D], BF16)) for i in range(4)]

    CN = ["sum", "sq", "mean", "var", "rstd", "m2", "sum2", "sq2"]

    def col(name):
        i = CN.index(name)
        return cols[:, i:i + 1]

    QN2 = cols[:, 8:16]
    RSTD8 = cols[:, 16:24]
    TMP0 = cols[:, 24:25]

    def ralloc(name, shape, dtype):
        return nc.sbuf_tensor(name, shape, dtype, side="right")

    xrt_stacks = {0: ExitStack(), 1: ExitStack()}

    for dst, src in [(idF, identF), (idB, identB), (onB, onesB), (onF, onesF),
                     (trL, triNL), (trU, triNU), (hb_col, hbias),
                     (la_sb, la), (cm_col, cmask)]:
        kb.ldy(lambda y, d=dst, s=src: y.dma_start(out=d[:, :], in_=s.ap()))
    wl = kb.dy.at()

    kb.v(lambda v: v.memset(cst[:, 0:1], 0.5))
    kb.v(lambda v: v.memset(cst[:, 1:2], 1e-24))
    kb.v(lambda v: v.memset(cst[:, 2:3], -1.0))
    cst_tag = kb.v(lambda v: v.memset(cst[:, 3:4], 1e-5))
    C05 = cst[:, 0:1]; CEM24 = cst[:, 1:2]; CM1 = cst[:, 2:3]; CEPS = cst[:, 3:4]

    def ln_stats(srcs, nelem, extra_waits=(), eps=1e-5):
        first = True
        rtags = []
        for ap_fn, scr_fn in srcs:
            rtags.append(kb.v(lambda v, f=ap_fn, first=first: v.tensor_reduce(
                col("sum") if first else col("sum2"), f(), AX.X, ALU.add),
                waits=extra_waits))
            kb.s(lambda s, f=ap_fn, g=scr_fn, first=first: s.activation(
                g(), f(), AF.Square,
                accum_out=col("sq") if first else col("sq2")),
                waits=tuple(extra_waits) + (kb.vv.at(),))
            first = False
        if len(srcs) > 1:
            sum_tag = kb.v(lambda v: v.tensor_add(
                col("sum"), col("sum"), col("sum2")), waits=[rtags[-1]])
            sq_tag = kb.v(lambda v: v.tensor_add(
                col("sq"), col("sq"), col("sq2")), waits=[kb.ss.at()])
        else:
            sum_tag = rtags[0]
            sq_tag = kb.ss.at()
        vh = kb.v(lambda v: v.tensor_scalar(col("var"), col("sq"), 1.0 / nelem,
                                            0.0, ALU.mult, ALU.add),
                  waits=[sq_tag])
        mn = kb.v(lambda v: v.tensor_scalar(col("mean"), col("sum"),
                                            1.0 / nelem, 0.0, ALU.mult,
                                            ALU.add), waits=[sum_tag])
        nv = kb.v(lambda v: v.scalar_tensor_tensor(
            col("var"), col("mean"), col("mean"), col("var"),
            ALU.mult, ALU.subtract), waits=[mn, vh])
        sq_t = kb.s(lambda s: s.activation(col("rstd"), col("var"), AF.Sqrt,
                                           bias=CEPS, scale=-1.0),
                    waits=[nv, cst_tag])
        return kb.v(lambda v: v.reciprocal(col("rstd"), col("rstd")),
                    waits=[sq_t])

    with ExitStack() as es_ssm:
        S = es_ssm.enter_context
        ps_tr = S(nc.psum_tensor("ps_tr", [P, 2 * 512], F32))
        og = S(nc.sbuf_tensor("og", [P, ND, T], BF16))
        ones5 = S(nc.sbuf_tensor("ones5", [P, 512], F32))
        xt = [fA[0], fA[1]]
        gPb1 = S(nc.sbuf_tensor("gPb1", [P, T], BF16))
        gPb = [gPb1, gPb1]

        cm_tag = wl
        st = kb.s(lambda s: s.activation(a_col[:, :], la_sb[:, :], AF.Tanh,
                                         scale=C05), waits=[wl, cst_tag])
        acol_tag = kb.v(lambda v: v.tensor_scalar(a_col[:, :], a_col[:, :],
                                                  0.5, 0.5, ALU.mult, ALU.add),
                        waits=[st])
        kb.v(lambda v: v.memset(ones5[:, :], 1.0))

        with ExitStack() as es12:
            S2 = es12.enter_context
            ps_z = S2(nc.psum_tensor("ps_z", [P, 4 * 512], F32))
            xnT = S2(nc.sbuf_tensor("xnT", [P, ND, T], BF16))
            zw = [[S2(nc.sbuf_tensor(f"zw{pq}_{i}", [P, ND, P], BF16))
                   for i in range(4)] for pq in range(2)]

            wsrc = [wdT, wsT, wiT, wgT]
            wtag = {}
            for m in range(2):
                for i in range(4):
                    wtag[(m, i)] = kb.ldg(lambda g, i=i, m=m: g.dma_start(
                        out=zw[m % 2][i][:, :, :],
                        in_=wsrc[i].ap()[:, m * P:(m + 1) * P]
                        .rearrange("(a p) c -> p a c", p=P)))

            xload = {}
            for i in range(2):
                xload[i] = kb.ldy(lambda y, i=i: y.dma_start(
                    out=xt[i % 2][:, :], in_=x_in.ap()[i * P:(i + 1) * P, :]))
            trg = [None, None]
            fa5_free = None
            for i in range(NT):
                pq = i % 2
                src = xt[pq]
                stag = ln_stats([(lambda src=src: src[:, :],
                                  lambda: fA[4][:, :])], D,
                                extra_waits=[xload[i]])
                xn_tag = kb.v(lambda v, src=src: v.tensor_scalar(
                    fA[5][:, :], src[:, :], col("mean"), col("rstd"),
                    ALU.subtract, ALU.mult), waits=[stag, fa5_free])
                if i + 2 < NT:
                    xload[i + 2] = kb.ldy(lambda y, i=i: y.dma_start(
                        out=xt[i % 2][:, :],
                        in_=x_in.ap()[(i + 2) * P:(i + 3) * P, :]),
                        waits=[xn_tag])
                for k in range(ND):
                    half = k // 4
                    slot = k % 4
                    kb.t(lambda t, half=half, slot=slot, k=k: t.transpose(
                        ps_tr[:, half * 512 + slot * P:half * 512 + (slot + 1) * P],
                        fA[5][:, k * P:(k + 1) * P], idF[:, :]),
                        waits=[xn_tag, trg[half], wl],
                        mark=(k % 4 == 3))
                    if k % 4 == 3:
                        tw = kb.tt.at()
                        ks0 = k - 3
                        trg[half] = kb.v(lambda v, i=i, ks0=ks0, half=half: v.tensor_copy(
                            xnT[:, ks0:ks0 + 4, i * P:(i + 1) * P],
                            ps_tr[:, half * 512:(half + 1) * 512]
                            .rearrange("p (a b) -> p a b", b=P)), waits=[tw])
                fa5_free = kb.tt.at()

            kb.tap_raw(nc, "xnT", lambda: xnT[:, :, :], [P, ND, T], BF16,
                       [kb.vv.at()])
            kb.barrier()

            zguard = None
            gp_guard = [None, None]
            for m in range(ND):
                wpar = m % 2
                wt = [wtag[(m, i)] for i in range(4)]
                for c in range(4):
                    cs = slice(c * 512, (c + 1) * 512)
                    ztags = []
                    for zi in range(4):
                        for k in range(ND):
                            w = [wt[zi], zguard] if k == 0 else ()
                            kb.t(lambda t, zi=zi, k=k, wpar=wpar, cs=cs: t.matmul(
                                ps_z[:, zi * 512:(zi + 1) * 512],
                                zw[wpar][zi][:, k, :], xnT[:, k, cs],
                                start=(k == 0), stop=(k == ND - 1)),
                                waits=w, mark=(k == ND - 1))
                        ztags.append(kb.tt.at())
                    ed = fA[0][:, 0:512]; sp = fA[0][:, 512:1024]
                    ths = fA[1][:, 0:512]; thi = fA[1][:, 512:1024]
                    dec = fA[2][:, 0:512]
                    upd = fA[3][:, 0:512]
                    sc = c % 2
                    stt_ = fA[4][:, sc * 512:(sc + 1) * 512]
                    Pt = fA[5][:, sc * 512:(sc + 1) * 512]
                    stt_prev = fA[4][:, (1 - sc) * 512 + 511:(1 - sc) * 512 + 512]
                    Pt_prev = fA[5][:, (1 - sc) * 512 + 511:(1 - sc) * 512 + 512]
                    kb.s(lambda s, ed=ed: s.activation(ed, ps_z[:, 0:512],
                                                       AF.Exp),
                         waits=[ztags[0], kb.vv.at()])
                    sp_tag = kb.s(lambda s, ed=ed, sp=sp: s.activation(
                        sp, ed, AF.Ln, bias=1.0))
                    kb.s(lambda s, ths=ths: s.activation(
                        ths, ps_z[:, 512:1024], AF.Tanh, scale=C05),
                        waits=[ztags[1]])
                    th_tag = kb.s(lambda s, thi=thi: s.activation(
                        thi, ps_z[:, 1024:1536], AF.Tanh), waits=[ztags[2]])
                    sv = kb.v(lambda v, sp=sp, m=m: v.tensor_scalar(
                        sp, sp, 1e-4, a_col[:, m:m + 1], ALU.max, ALU.mult),
                        waits=[sp_tag, acol_tag])
                    dec_tag = kb.s(lambda s, dec=dec, sp=sp: s.activation(
                        dec, sp, AF.Exp, scale=CM1), waits=[sv])
                    kb.v(lambda v, upd=upd, ths=ths, thi=thi: v.scalar_tensor_tensor(
                        upd, ths, 1.0, thi, ALU.add, ALU.mult), waits=[th_tag])
                    init_s = 0.0 if c == 0 else stt_prev
                    kb.v(lambda v, stt_=stt_, dec=dec, upd=upd, init_s=init_s:
                         v.tensor_tensor_scan(stt_, dec, upd, init_s,
                                              ALU.mult, ALU.add),
                         waits=[dec_tag])
                    init_p = 1.0 if c == 0 else Pt_prev
                    kb.v(lambda v, Pt=Pt, dec=dec, init_p=init_p:
                         v.tensor_tensor_scan(Pt, dec, ones5[:, :], init_p,
                                              ALU.mult, ALU.mult))
                    thg = fA[0][:, 0:512]
                    thg_tag = kb.s(lambda s, thg=thg: s.activation(
                        thg, ps_z[:, 1536:2048], AF.Tanh, scale=C05),
                        waits=[ztags[3]])
                    zguard = kb.ss.at()
                    kb.v(lambda v, m=m, cs=cs, thg=thg, stt_=stt_:
                         v.scalar_tensor_tensor(og[:, m, cs], thg, 1.0, stt_,
                                                ALU.add, ALU.mult),
                         waits=[thg_tag])
                    kb.v(lambda v, cs=cs, thg=thg, Pt=Pt, wpar=wpar:
                         v.scalar_tensor_tensor(gPb[wpar][:, cs], thg, 1.0, Pt,
                                                ALU.add, ALU.mult),
                         waits=[gp_guard[wpar]])
                    if c == 3:
                        kb.v(lambda v, m=m, stt_=stt_: v.tensor_copy(
                            fin[:, m:m + 1], stt_[:, 511:512]))
                gw = kb.ldy(lambda y, m=m: y.dma_start(
                    out=gP_d.ap()[m, :, :], in_=gPb[m % 2][:, :]),
                    waits=[kb.vv.at()])
                gp_guard[0] = gw
                gp_guard[1] = gw
                if m + 2 < ND:
                    for i in range(4):
                        wtag[(m + 2, i)] = kb.ldg(lambda g, i=i, m=m: g.dma_start(
                            out=zw[m % 2][i][:, :, :],
                            in_=wsrc[i].ap()[:, (m + 2) * P:(m + 3) * P]
                            .rearrange("(a p) c -> p a c", p=P)),
                            waits=[kb.tt.at()])

            fin_tag = kb.vv.at()
            kb.tap_raw(nc, "og0", lambda: og[:, :, :], [P, ND, T], BF16,
                       [fin_tag])
            kb.tap_raw(nc, "fin", lambda: fin[:, :], [P, ND], F32, [fin_tag])
            kb.tap_raw(nc, "acol", lambda: a_col[:, :], [P, ND], F32, [fin_tag])
            t1 = kb.t(lambda t: t.transpose(ps_tr[0:ND, 0:P], fin[:, :],
                                            idF[:, :]),
                      waits=[fin_tag], mark=True)
            c1 = kb.v(lambda v: v.tensor_copy(fA[0][0:ND, 0:P],
                                              ps_tr[0:ND, 0:P]), waits=[t1])
            d1 = kb.ldg(lambda g: g.dma_start(out=ag1i.ap(),
                                              in_=fA[0][0:ND, 0:P]), waits=[c1])
            kb.op("g", lambda g: g.collective_compute(
                "AllGather", ALU.bypass, replica_groups=PAIRS,
                ins=[ag1i.ap().opt()], outs=[ag1o.ap().opt()]),
                waits=[d1], incs=[kb.cc.inc()])
            d2 = kb.ldg(lambda g: g.dma_start(out=fA[0][0:ND, 0:P],
                                              in_=ag1o.ap()[0:ND, :]),
                        waits=[kb.cc.at()])
            t2 = kb.t(lambda t: t.transpose(ps_tr[:, 0:ND], fA[0][0:ND, 0:P],
                                            idF[0:ND, 0:ND]), waits=[d2], mark=True)
            car_tag = kb.v(lambda v: v.tensor_scalar(
                carry[:, :], ps_tr[:, 0:ND], cm_col[:, :], 0.0, ALU.mult, ALU.add),
                waits=[t2, cm_tag])
            rb = kb.ldy(lambda y: y.dma_start(
                out=gPb[0][:, :], in_=gP_d.ap()[0, :, :]),
                waits=[gp_guard[0]])
            for m in range(ND):
                fx = kb.v(lambda v, m=m: v.scalar_tensor_tensor(
                    og[:, m, :], gPb[0][:, :], carry[:, m:m + 1],
                    og[:, m, :], ALU.mult, ALU.add),
                    waits=[rb, car_tag])
                if m + 1 < ND:
                    rb = kb.ldy(lambda y, m=m: y.dma_start(
                        out=gPb[0][:, :], in_=gP_d.ap()[m + 1, :, :]),
                        waits=[fx])
            kb.tap_raw(nc, "og1", lambda: og[:, :, :], [P, ND, T], BF16,
                       [kb.vv.at()])
            kb.tap_raw(nc, "carry", lambda: carry[:, :], [P, ND], F32,
                       [kb.vv.at()])
            kb.barrier()

        x_resT = xrt_stacks[0].enter_context(ralloc("x_resT0", [P, ND, T + 1], BF16))
        with ExitStack() as es3:
            S3 = es3.enter_context
            ps_y = S3(nc.psum_tensor("ps_y", [P, 2 * D], F32))
            woutT = S3(nc.sbuf_tensor("woutT", [P, ND, D], BF16))
            wout_tag = kb.ldg(lambda g: g.dma_start(
                out=woutT[:, :, :],
                in_=woT.ap().rearrange("(a p) c -> p a c", p=P)))
            wsc = kb.v(lambda v: v.tensor_scalar(
                woutT[:, :, :].rearrange("p a b -> p (a b)"),
                woutT[:, :, :].rearrange("p a b -> p (a b)"),
                0.25, 0.0, ALU.mult, ALU.add), waits=[wout_tag])
            x1b = [fA[2], fA[3]]
            xload = {}
            for i in range(2):
                xload[i] = kb.ldy(lambda y, i=i: y.dma_start(
                    out=xt[i % 2][:, :], in_=x_in.ap()[i * P:(i + 1) * P, :]))
            yguard = [None, None]
            x1w = [None, None]
            x1guard = [None, None]
            trg = [None, None]
            for i in range(NT):
                pq = i % 2
                yp = ps_y[:, pq * D:(pq + 1) * D]
                for k in range(ND):
                    for n in range(2):
                        w = [yguard[pq], wsc] if (k == 0 and n == 0) else ()
                        kb.t(lambda t, i=i, k=k, n=n, yp=yp: t.matmul(
                            yp[:, n * 512:(n + 1) * 512],
                            og[:, k, i * P:(i + 1) * P],
                            woutT[:, k, n * 512:(n + 1) * 512],
                            start=(k == 0), stop=(k == ND - 1)),
                            waits=w, mark=(k == ND - 1 and n == 1))
                ytag = kb.tt.at()
                stag = ln_stats([(lambda yp=yp: yp[:, :],
                                  lambda: fA[4][:, :])], D,
                                extra_waits=[ytag])
                t1tag = kb.v(lambda v, yp=yp: v.tensor_scalar(
                    fA[5][:, :], yp[:, :], col("mean"), col("rstd"),
                    ALU.subtract, ALU.mult), waits=[stag])
                yguard[pq] = t1tag
                x1tag = kb.v(lambda v, pq=pq: v.tensor_add(
                    x1b[pq][:, :], xt[pq][:, :], fA[5][:, :]),
                    waits=[xload[i], x1w[pq], x1guard[pq]])
                if i + 2 < NT:
                    xload[i + 2] = kb.ldy(lambda y, i=i: y.dma_start(
                        out=xt[i % 2][:, :],
                        in_=x_in.ap()[(i + 2) * P:(i + 3) * P, :]),
                        waits=[x1tag])
                x1w[pq] = kb.ldy(lambda y, i=i, pq=pq: y.dma_start(
                    out=x1_d.ap()[i * P:(i + 1) * P, :], in_=x1b[pq][:, :]),
                    waits=[x1tag])
                stag = ln_stats([(lambda pq=pq: x1b[pq][:, :],
                                  lambda: fA[4][:, :])], D)
                xf_tag = kb.v(lambda v, i=i, pq=pq: v.tensor_scalar(
                    x_res[:, i, :], x1b[pq][:, :], col("mean"), col("rstd"),
                    ALU.subtract, ALU.mult), waits=[stag])
                x1guard[pq] = xf_tag
                for k in range(ND):
                    half = k // 4
                    slot = k % 4
                    kb.t(lambda t, i=i, k=k, half=half, slot=slot: t.transpose(
                        ps_tr[:, half * 512 + slot * P:half * 512 + (slot + 1) * P],
                        x_res[:, i, k * P:(k + 1) * P], idF[:, :]),
                        waits=[xf_tag, trg[half]], mark=(k % 4 == 3))
                    if k % 4 == 3:
                        tw = kb.tt.at()
                        ks0 = k - 3
                        trg[half] = kb.v(lambda v, i=i, ks0=ks0, half=half,
                                         xr=x_resT: v.tensor_copy(
                            xr[:, ks0:ks0 + 4, 1 + i * P:1 + (i + 1) * P],
                            ps_tr[:, half * 512:(half + 1) * 512]
                            .rearrange("p (a b) -> p a b", b=P)), waits=[tw])

    kb.barrier()

    kb.tap_xres(nc, "xf", x_res, [kb.vv.at()])

    def ag_row(l, agi, ago, src_tag):
        d1 = kb.ldg(lambda g, agi=agi: g.dma_start(
            out=agi.ap(), in_=x_res[P - 1:P, NT - 1, :]), waits=[src_tag])
        kb.op("g", lambda g, agi=agi, ago=ago: g.collective_compute(
            "AllGather", ALU.bypass, replica_groups=PAIRS,
            ins=[agi.ap().opt()], outs=[ago.ap().opt()]),
            waits=[d1], incs=[kb.cc.inc()])
        return kb.cc.at()

    def emit_layer(l, x_resT):
        if l == 1:
            x_resT = xrt_stacks[1].enter_context(ralloc("x_resT1", [P, ND, T + 1], BF16))
            with ExitStack() as esr:
                ps_rb = esr.enter_context(nc.psum_tensor("ps_rb", [P, 2 * 512], F32))
                trg = [None, None]
                for i in range(NT):
                    for k in range(ND):
                        half = k // 4
                        slot = k % 4
                        kb.t(lambda t, i=i, k=k, half=half, slot=slot: t.transpose(
                            ps_rb[:, half * 512 + slot * P:
                                  half * 512 + (slot + 1) * P],
                            x_res[:, i, k * P:(k + 1) * P], idF[:, :]),
                            waits=[trg[half]], mark=(k % 4 == 3))
                        if k % 4 == 3:
                            tw = kb.tt.at()
                            ks0 = k - 3
                            trg[half] = kb.v(lambda v, i=i, ks0=ks0, half=half:
                                             v.tensor_copy(
                                x_resT[:, ks0:ks0 + 4, 1 + i * P:1 + (i + 1) * P],
                                ps_rb[:, half * 512:(half + 1) * 512]
                                .rearrange("p (a b) -> p a b", b=P)),
                                waits=[tw])
                kb.barrier()

        with ExitStack() as esl:
            SL = esl.enter_context
            qT = SL(nc.sbuf_tensor(f"qT{l}", [P, 4, T], BF16))
            kT = SL(nc.sbuf_tensor(f"kT{l}", [P, 4, T], BF16))
            v_sb = SL(nc.sbuf_tensor(f"v_sb{l}", [P, NT, INNER], BF16))
            effc = SL(nc.sbuf_tensor(f"effc{l}", [P, NT, DH], BF16))
            effs = SL(nc.sbuf_tensor(f"effs{l}", [P, NT, DH], BF16))
            khalo = SL(nc.sbuf_tensor(f"khalo{l}", [P, 4, 512], BF16))
            vhalo = SL(nc.sbuf_tensor(f"vhalo{l}", [P, 4, 512], BF16))

            with ExitStack() as esq:
                SQ = esq.enter_context
                wsl = SQ(nc.sbuf_tensor(f"wsl{l}", [P, ND, 512], BF16))
                ps_qk = SQ(nc.psum_tensor(f"ps_qk{l}", [P, 2 * 512], F32))
                ps_t2 = SQ(nc.psum_tensor(f"ps_t2{l}", [P, 2 * 512], F32))
                ps_tb = [SQ(nc.psum_tensor(f"ps_tb{l}_{i}", [P, 512], BF16))
                         for i in range(2)]

                qsb_tag = kb.ldg(lambda g: g.dma_start(out=qs_b[:, :],
                                                       in_=qs_w[l].ap()))
                ksb_tag = kb.ldg(lambda g: g.dma_start(out=ks_b[:, :],
                                                       in_=ks_w[l].ap()))
                for dst, srcw in [(lng_c, lng_w[l]), (lnb_c, lnb_w[l]),
                                  (b1a_c, b1a_w[l]), (b1g_c, b1g_w[l])]:
                    kb.ldy(lambda y, d=dst, s=srcw: y.dma_start(out=d[:, :],
                                                                in_=s.ap()))
                lnb_tag = kb.dy.at()
                b1_tag = kb.dy.at()

                hw = ag_row(l, ag2i[l], ag2o[l], kb.vv.at())
                hd = kb.ldg(lambda g: g.dma_start(out=fA[0][0:1, :],
                                                  in_=ag2o[l].ap()[0:1, :]),
                            waits=[hw])
                for k in range(4, 8):
                    kb.t(lambda t, k=k: t.transpose(
                        ps_t2[:, (k - 4):(k - 3)],
                        fA[0][0:1, k * P:(k + 1) * P], idF[0:1, 0:1]),
                        waits=[hd] if k == 4 else (), mark=(k == 7))
                tw = kb.tt.at()
                kb.v(lambda v: v.tensor_scalar(
                    x_resT[:, 4:8, 0:1],
                    ps_t2[:, 0:4].rearrange("p (a b) -> p a b", b=1),
                    cm_col[:, :], 0.0, ALU.mult, ALU.add),
                    waits=[tw, cm_tag])
                xrt_tag = kb.v(lambda v: v.memset(x_resT[:, 0:4, 0:1], 0.0))

                kb.tap_raw(nc, f"xrt{l}", lambda: x_resT[:, :, :],
                           [P, ND, T + 1], BF16, [xrt_tag])
                kb.barrier()

                def build_eff(src_c, src_s, sc_b, after):
                    t1 = kb.ldg(lambda g, src_c=src_c: g.dma_start(
                        out=effc[:, :, :],
                        in_=src_c.ap().rearrange("(a p) b -> p a b", p=P)),
                        waits=[after])
                    t2 = kb.ldg(lambda g, src_s=src_s: g.dma_start(
                        out=effs[:, :, :],
                        in_=src_s.ap().rearrange("(a p) b -> p a b", p=P)),
                        waits=[after])
                    kb.v(lambda v, sc_b=sc_b: v.tensor_mul(
                        effc[:, :, :], effc[:, :, :],
                        sc_b[:, :].unsqueeze(1).broadcast_to([P, NT, DH])),
                        waits=[t1])
                    kb.v(lambda v, sc_b=sc_b: v.tensor_mul(
                        effs[:, :, 0:32], effs[:, :, 0:32],
                        sc_b[:, 32:64].unsqueeze(1).broadcast_to([P, NT, 32])),
                        waits=[t2])
                    return kb.v(lambda v, sc_b=sc_b: v.tensor_mul(
                        effs[:, :, 32:64], effs[:, :, 32:64],
                        sc_b[:, 0:32].unsqueeze(1).broadcast_to([P, NT, 32])))

                def qkv_pass(which, wcol, outT, eff_tag):
                    wload = kb.ldg(lambda g: g.dma_start(
                        out=wsl[:, :, :],
                        in_=wqkv[l].ap()[:, wcol:wcol + 512]
                        .rearrange("(a p) c -> p a c", p=P)),
                        waits=[kb.tt.at()])
                    kb.tap_raw(nc, f"wsl_{which}{l}", lambda: wsl[:, :, :],
                               [P, ND, 512], BF16, [wload])
                    pguard = [None, None]
                    trg2 = [None, None]
                    qb_free = [None, None]
                    ab_free = [None, None]
                    last = None
                    for i in range(NT):
                        pq = i % 2
                        pp = ps_qk[:, pq * 512:(pq + 1) * 512]
                        for k in range(ND):
                            off = 1 + i * P - (1 if k >= 4 else 0)
                            w = [wload, xrt_tag, pguard[pq]] if k == 0 else ()
                            kb.t(lambda t, k=k, off=off, pp=pp: t.matmul(
                                pp[:, :], x_resT[:, k, off:off + P],
                                wsl[:, k, :], start=(k == 0),
                                stop=(k == ND - 1)),
                                waits=w, mark=(k == ND - 1))
                        mtag = kb.tt.at()
                        if which == "v":
                            pguard[pq] = kb.v(lambda v, i=i, pp=pp: v.tensor_copy(
                                v_sb[:, i, :], pp[:, :]), waits=[mtag])
                            last = pguard[pq]
                            continue
                        qb = bA[0][:, pq * 512:(pq + 1) * 512]
                        pguard[pq] = kb.v(lambda v, pp=pp, qb=qb: v.tensor_copy(
                            qb, pp[:, :]), waits=[mtag, qb_free[pq]])
                        sq = fA[4][:, pq * 512:(pq + 1) * 512]
                        kb.v(lambda v, qb=qb, sq=sq: v.tensor_mul(sq, qb, qb))
                        qn2_tag = kb.v(lambda v, sq=sq: v.tensor_reduce(
                            QN2, sq.rearrange("p (a b) -> p a b", b=DH),
                            AX.X, ALU.add), waits=[kb.gg.at()])
                        sqr_tag = kb.s(lambda s: s.activation(
                            RSTD8, QN2, AF.Sqrt, bias=CEM24),
                            waits=[qn2_tag, cst_tag])
                        rstd_tag = kb.v(lambda v: v.reciprocal(RSTD8, RSTD8),
                                        waits=[sqr_tag])
                        qn = bA[1][:, pq * 512:(pq + 1) * 512]
                        qn_tag = kb.g(lambda g, qb=qb, qn=qn: g.tensor_tensor(
                            qn.rearrange("p (a b) -> p a b", b=DH),
                            qb.rearrange("p (a b) -> p a b", b=DH),
                            RSTD8.unsqueeze(2).broadcast_to([P, 8, DH]),
                            ALU.mult), waits=[rstd_tag])
                        qb_free[pq] = qn_tag
                        at = bA[2][:, pq * 512:(pq + 1) * 512]
                        bt = bA[3][:, pq * 512:(pq + 1) * 512]
                        kb.v(lambda v, qn=qn, at=at, i=i: v.tensor_mul(
                            at.rearrange("p (a b) -> p a b", b=DH),
                            qn.rearrange("p (a b) -> p a b", b=DH),
                            effc[:, i, :].unsqueeze(1).broadcast_to([P, 8, DH])),
                            waits=[qn_tag, eff_tag, ab_free[pq]])
                        kb.v(lambda v, qn=qn, bt=bt, i=i: v.tensor_mul(
                            bt.rearrange("p (a b) -> p a b", b=DH)[:, :, 0:32],
                            qn.rearrange("p (a b) -> p a b", b=DH)[:, :, 32:64],
                            effs[:, i, 0:32].unsqueeze(1).broadcast_to([P, 8, 32])))
                        kb.v(lambda v, qn=qn, bt=bt, i=i: v.tensor_mul(
                            bt.rearrange("p (a b) -> p a b", b=DH)[:, :, 32:64],
                            qn.rearrange("p (a b) -> p a b", b=DH)[:, :, 0:32],
                            effs[:, i, 32:64].unsqueeze(1).broadcast_to([P, 8, 32])))
                        qp_tag = kb.v(lambda v, at=at, bt=bt: v.tensor_add(
                            at, at, bt))
                        for h2 in range(4):
                            half = h2 % 2
                            tw = kb.t(lambda t, at=at, h2=h2, half=half:
                                      t.transpose(
                                ps_tb[half][:, 0:P],
                                at[:, h2 * P:(h2 + 1) * P], idB[:, :]),
                                waits=[qp_tag, trg2[half]], mark=True)
                            trg2[half] = kb.v(lambda v, i=i, h2=h2, half=half,
                                              outT=outT: v.tensor_copy(
                                outT[:, h2, i * P:(i + 1) * P],
                                ps_tb[half][:, 0:P]),
                                waits=[tw])
                        ab_free[pq] = kb.tt.at()
                        last = trg2[0]
                    return kb.vv.at()

                eq = build_eff(csq, ssq, qs_b, qsb_tag)
                q_done = qkv_pass("q", 0, qT, eq)
                ek = build_eff(csk, ssk, ks_b, q_done)
                k_done = qkv_pass("k", 512, kT, ek)
                v_done = qkv_pass("v", 1024, None, None)
                kb.tap_raw(nc, f"q{l}", lambda: qT[:, :, :], [P, 4, T], BF16,
                           [q_done])
                kb.tap_raw(nc, f"k{l}", lambda: kT[:, :, :], [P, 4, T], BF16,
                           [k_done])
                kb.tap_raw(nc, f"v{l}", lambda: v_sb[:, :, :],
                           [P, NT, INNER], BF16, [v_done])

                d1 = kb.ldg(lambda g: g.dma_start(
                    out=kvi[l].ap()[:, 0:2048],
                    in_=kT[:, :, 1536:2048]),
                    waits=[k_done])
                d1b = kb.ldg(lambda g: g.dma_start(
                    out=kvi[l].ap()[:, 2048:4096],
                    in_=v_sb[:, 12:16, :]),
                    waits=[v_done])
                kb.op("g", lambda g: g.collective_compute(
                    "AllGather", ALU.bypass, replica_groups=PAIRS,
                    ins=[kvi[l].ap().opt()], outs=[kvo[l].ap().opt()]),
                    waits=[d1, d1b], incs=[kb.cc.inc()])
                kh_tag = kb.ldg(lambda g: g.dma_start(
                    out=khalo[:, :, :].rearrange("p a b -> p (a b)"),
                    in_=kvo[l].ap()[0:P, 0:2048]), waits=[kb.cc.at()])
                halo_kv = kb.ldg(lambda g: g.dma_start(
                    out=vhalo[:, :, :].rearrange("p a b -> p (a b)"),
                    in_=kvo[l].ap()[0:P, 2048:4096]))
                kb.tap_raw(nc, f"kh{l}", lambda: khalo[:, :, :],
                           [P, 4, 512], BF16, [kh_tag])
                kb.tap_raw(nc, f"vh{l}", lambda: vhalo[:, :, :],
                           [P, 4, 512], BF16, [halo_kv])
                kb.barrier()
            xrt_stacks[l].close()

            oT_es = ExitStack()
            oT = oT_es.enter_context(ralloc(f"oT{l}", [P, 4, T], BF16))
            with ExitStack() as esa:
                SA = esa.enter_context
                eT = SA(nc.sbuf_tensor(f"eT{l}", [P, 8, 512], BF16))
                rdb_sb = SA(nc.sbuf_tensor(f"rdb{l}", [P, 512], F32))
                msk = SA(nc.sbuf_tensor(f"msk{l}", [P, 640], F32))
                msk0 = SA(nc.sbuf_tensor(f"msk0{l}", [P, 4, 640], F32))
                ps_mx = SA(nc.psum_tensor(f"ps_mx{l}", [P, 2 * D], F32))
                ps_sT = SA(nc.psum_tensor(f"ps_sT{l}", [P, 2 * 512], F32))
                ps_av = SA(nc.psum_tensor(f"ps_av{l}", [P, 512], F32))
                ps_dn = SA(nc.psum_tensor(f"ps_dn{l}", [P, 512], F32))

                kb.ldy(lambda y: y.dma_start(out=msk[:, :], in_=mask_std.ap()))
                m2 = kb.ldy(lambda y: y.dma_start(
                    out=msk0[:, :, :], in_=mask_w0.ap().transpose([1, 0, 2])))
                ez = kb.v(lambda v: v.memset(
                    eT[:, :, :].rearrange("p a b -> p (a b)"), 0.0))

                mx_guard = [None, None]
                sT_guard = [None, None]
                av_guard = None
                dn_guard = None
                eT_guard = None
                for w in range(NW):
                    for h in range(H):
                        hr = (h % 2) * DH
                        ht = h // 2
                        cpos = (0, hr)
                        for it in range(4):
                            mq = it % 2
                            mp = ps_mx[:, mq * D:(mq + 1) * D]
                            for jh in range(2):
                                def mmA(t, w=w, jh=jh, hr=hr, ht=ht, it=it, mp=mp):
                                    if w == 0 and jh == 0:
                                        rhs = khalo[hr:hr + DH, ht, 0:512]
                                    else:
                                        base = (w - 1 + jh) * 512
                                        rhs = kT[hr:hr + DH, ht, base:base + 512]
                                    return t.matmul(
                                        mp[:, jh * 512:(jh + 1) * 512],
                                        qT[hr:hr + DH, ht,
                                           w * 512 + it * P:w * 512 + (it + 1) * P],
                                        rhs, start=True, stop=True)
                                kb.t(mmA,
                                     waits=[mx_guard[mq], halo_kv] if jh == 0 else (),
                                     mark=(jh == 1))
                            mtag = kb.tt.at()
                            def ttr(v, w=w, it=it, mp=mp):
                                mask = msk0[:, it, :] if w == 0 else msk[:, :]
                                return v.tensor_add(
                                    fA[4][:, 0:640],
                                    mp[:, it * P:it * P + 640], mask)
                            mx_guard[mq] = kb.v(ttr, waits=[mtag, m2, kb.tt.at()])
                            red_tag = kb.v(lambda v: v.tensor_reduce(
                                TMP0, fA[4][:, 0:640], AX.X, ALU.max))
                            tw = kb.t(lambda t: t.transpose(
                                ps_dn[0:1, 0:P], TMP0, idF[:, :]),
                                waits=[red_tag, dn_guard], mark=True)
                            dn_guard = kb.v(lambda v, it=it: v.tensor_scalar(
                                nmrow[0:1, it * P:(it + 1) * P],
                                ps_dn[0:1, 0:P], -1.0, 0.0, ALU.mult, ALU.add),
                                waits=[tw])
                        nm_tag = dn_guard

                        jt_tags = []
                        for jt in range(8):
                            sq_ = jt % 2
                            sp_ = ps_sT[:, sq_ * 512:(sq_ + 1) * 512]
                            o = (jt % 4) * P
                            lo, hi = (0, o + P) if jt < 4 else (o, 512)
                            def mmB(t, w=w, jt=jt, hr=hr, ht=ht, sp_=sp_,
                                    lo=lo, hi=hi):
                                if w == 0 and jt < 4:
                                    lhs = khalo[hr:hr + DH, ht, jt * P:(jt + 1) * P]
                                elif jt < 4:
                                    base = (w - 1) * 512
                                    lhs = kT[hr:hr + DH, ht,
                                             base + jt * P:base + (jt + 1) * P]
                                else:
                                    base = w * 512
                                    lhs = kT[hr:hr + DH, ht,
                                             base + (jt - 4) * P:base + (jt - 3) * P]
                                return t.matmul(
                                    sp_[:, lo:hi], lhs,
                                    qT[hr:hr + DH, ht, w * 512 + lo:w * 512 + hi],
                                    start=True, stop=False)
                            kb.op("t", mmB, waits=[sT_guard[sq_]])
                            kb.t(lambda t, sp_=sp_, lo=lo, hi=hi: t.matmul(
                                sp_[:, lo:hi], onB[0:1, :], nmrow[0:1, lo:hi],
                                start=False, stop=True),
                                waits=[nm_tag], mark=True)
                            smt = kb.tt.at()
                            tri_tag = kb.v(lambda v, jt=jt, sp_=sp_, o=o:
                                           v.tensor_add(
                                sp_[:, o:o + P], sp_[:, o:o + P],
                                (trL if jt < 4 else trU)[:, :]), waits=[smt])
                            def ex(s, w=w, jt=jt, sp_=sp_, lo=lo, hi=hi):
                                bias = hb_col[:, :] if (w == 0 and jt < 4) else 0.0
                                return s.activation(eT[:, jt, lo:hi],
                                                    sp_[:, lo:hi], AF.Exp,
                                                    bias=bias)
                            sT_guard[sq_] = kb.s(ex, waits=[tri_tag, eT_guard, ez])
                            jt_tags.append(kb.ss.at())
                        for jt in range(8):
                            kb.t(lambda t, jt=jt: t.matmul(
                                ps_dn[0:1, 0:512], onB[:, 0:1], eT[:, jt, :],
                                start=(jt == 0), stop=(jt == 7)),
                                waits=[jt_tags[jt]] + ([dn_guard] if jt == 0 else []),
                                mark=(jt == 7))
                        den_tag = kb.tt.at()
                        for jt in range(8):
                            def amm(t, w=w, jt=jt, h=h, hr=hr, cpos=cpos):
                                if w == 0 and jt < 4:
                                    lhs = vhalo[:, jt, h * DH:(h + 1) * DH]
                                else:
                                    ti = (w - 1) * 4 + jt
                                    lhs = v_sb[:, ti, h * DH:(h + 1) * DH]
                                return t.matmul(ps_av[hr:hr + DH, :], lhs,
                                                eT[:, jt, :],
                                                start=(jt == 0), stop=(jt == 7),
                                                tile_position=cpos)
                            kb.t(amm, waits=[av_guard] if jt == 0 else (),
                                 mark=(jt == 7))
                        av_tag = kb.tt.at()
                        eT_guard = av_tag
                        dn_guard = kb.v(lambda v: v.tensor_copy(
                            den_s[0:1, :], ps_dn[0:1, 0:512]), waits=[den_tag])
                        rd_tag = kb.v(lambda v: v.reciprocal(rd_s[0:1, :],
                                                             den_s[0:1, :]),
                                      waits=[dn_guard])
                        rdb_tag = kb.t(lambda t, hr=hr, cpos=cpos: t.matmul(
                            ps_sT[hr:hr + DH, 0:512], onF[0:1, 0:DH],
                            rd_s[0:1, :], start=True, stop=True,
                            tile_position=cpos),
                            waits=[rd_tag, sT_guard[0]], mark=True)
                        rc = kb.v(lambda v, hr=hr: v.tensor_copy(
                            rdb_sb[hr:hr + DH, :], ps_sT[hr:hr + DH, 0:512]),
                            waits=[rdb_tag])
                        av_guard = kb.v(lambda v, w=w, hr=hr, ht=ht: v.tensor_mul(
                            oT[hr:hr + DH, ht, w * 512:(w + 1) * 512],
                            ps_av[hr:hr + DH, :], rdb_sb[hr:hr + DH, :]),
                            waits=[av_tag])
                        sT_guard[0] = rc
                att_done = kb.vv.at()
                kb.tap_raw(nc, f"o{l}", lambda: oT[:, :, :], [P, 4, T], BF16,
                           [att_done])
            kb.barrier()

        with ExitStack() as esw:
            SW = esw.enter_context
            wo_sb = SW(nc.sbuf_tensor(f"wo_sb{l}", [P, 4, D], BF16))
            ps_wo = SW(nc.psum_tensor(f"ps_wo{l}", [P, 2 * D], F32))
            wol = kb.ldg(lambda g: g.dma_start(
                out=wo_sb[:, :, :],
                in_=wo_w[l].ap().rearrange("(a p) c -> p a c", p=P)))
            guard = [None, None]
            for i in range(NT):
                pq = i % 2
                wp = ps_wo[:, pq * D:(pq + 1) * D]
                for kt in range(4):
                    for n in range(2):
                        w = [wol, att_done, guard[pq]] if (kt == 0 and n == 0) else ()
                        kb.t(lambda t, i=i, kt=kt, n=n, wp=wp: t.matmul(
                            wp[:, n * 512:(n + 1) * 512],
                            oT[:, kt, i * P:(i + 1) * P],
                            wo_sb[:, kt, n * 512:(n + 1) * 512],
                            start=(kt == 0), stop=(kt == 3)),
                            waits=w, mark=(kt == 3 and n == 1))
                mtag = kb.tt.at()
                guard[pq] = kb.v(lambda v, i=i, wp=wp: v.tensor_add(
                    x_res[:, i, :], x_res[:, i, :], wp[:, :]), waits=[mtag])
            xmid_tag = kb.vv.at()
        oT_es.close()
        kb.tap_xres(nc, f"xm{l}", x_res, [xmid_tag])
        kb.barrier()
        halo4 = ag_row(l, ag4i[l], ag4o[l], xmid_tag)
        kb.barrier()

        hn_es = ExitStack()
        hnT = hn_es.enter_context(ralloc(f"hnT{l}", [P, ND, T], BF16))
        with ExitStack() as esn:
            SN = esn.enter_context
            h2h = SN(nc.sbuf_tensor(f"h2h{l}", [P, NT, 512], F32))
            ps_h = SN(nc.psum_tensor(f"ps_h{l}", [P, 2 * 512], F32))
            s1 = kb.ldy(lambda y: y.dma_start(
                out=h2h[1:P, :, :], in_=x_res[0:P - 1, :, 512:1024]))
            s2 = kb.ldy(lambda y: y.dma_start(
                out=h2h[0:1, 1:NT, :], in_=x_res[P - 1:P, 0:NT - 1, 512:1024]))
            s3 = kb.ldg(lambda g: g.dma_start(
                out=h2h[0:1, 0:1, :].rearrange("p a b -> p (a b)"),
                in_=ag4o[l].ap()[0:1, 512:1024]), waits=[halo4])
            sh = kb.v(lambda v: v.tensor_scalar(
                h2h[0:1, 0:1, :].rearrange("p a b -> p (a b)"),
                h2h[0:1, 0:1, :].rearrange("p a b -> p (a b)"),
                cm_col[0:1, :], 0.0, ALU.mult, ALU.add), waits=[s3, s1, s2])
            trg = [None, None]
            fa5_free = None
            for i in range(NT):
                stag = ln_stats(
                    [(lambda i=i: x_res[:, i, 0:512], lambda: fA[4][:, 0:512]),
                     (lambda i=i: h2h[:, i, :], lambda: fA[4][:, 512:1024])],
                    D, extra_waits=[sh])
                kb.v(lambda v, i=i: v.tensor_scalar(
                    fA[5][:, 0:512], x_res[:, i, 0:512], col("mean"),
                    col("rstd"), ALU.subtract, ALU.mult),
                    waits=[stag, fa5_free])
                t1_tag = kb.v(lambda v, i=i: v.tensor_scalar(
                    fA[5][:, 512:1024], h2h[:, i, :], col("mean"),
                    col("rstd"), ALU.subtract, ALU.mult))
                for k in range(ND):
                    half = k // 4
                    tw = kb.t(lambda t, k=k, half=half: t.transpose(
                        ps_h[:, half * 512 + (k % 4) * P:
                             half * 512 + (k % 4) * P + P],
                        fA[5][:, k * P:(k + 1) * P], idF[:, :]),
                        waits=[t1_tag, trg[half]], mark=True)
                    trg[half] = kb.v(lambda v, i=i, k=k, half=half: v.tensor_scalar(
                        hnT[:, k, i * P:(i + 1) * P],
                        ps_h[:, half * 512 + (k % 4) * P:
                             half * 512 + (k % 4) * P + P],
                        lng_c[:, k:k + 1], lnb_c[:, k:k + 1],
                        ALU.mult, ALU.add), waits=[tw, lnb_tag])
                fa5_free = kb.tt.at()
            hn_tag = kb.vv.at()
        kb.barrier()

        with ExitStack() as es8:
            S8 = es8.enter_context
            geg = S8(nc.sbuf_tensor(f"geg{l}", [P, 11, T], BF16))
            b2_sb = bA[0]
            b2_tag = kb.ldg(lambda g: g.dma_start(out=b2_sb[0:1, :],
                                                  in_=b2_w[l].ap()))
            w2h = S8(nc.sbuf_tensor(f"w2h{l}", [P, 11, D], BF16))
            wa_sb = S8(nc.sbuf_tensor(f"wa{l}", [P, ND, P], BF16))
            wg_sb = S8(nc.sbuf_tensor(f"wg{l}", [P, ND, P], BF16))
            ps_ag = S8(nc.psum_tensor(f"ps_ag{l}", [P, 2 * 512], F32))
            ps_w2 = S8(nc.psum_tensor(f"ps_w2{l}", [P, 2 * D], F32))

            w2read = None
            for half in range(2):
                hts = HALVES[half]
                nloc = len(hts)
                rows0 = hts[0] * P
                nrows = min(HID, (hts[-1] + 1) * P) - rows0
                full = nrows // P
                wtags = [kb.ldg(lambda g, rows0=rows0, full=full: g.dma_start(
                    out=w2h[:, 0:full, :],
                    in_=w2_w[l].ap()[rows0:rows0 + full * P, :]
                    .rearrange("(a p) c -> p a c", p=P)),
                    waits=[w2read])]
                if nrows % P:
                    rem = nrows % P
                    zz = kb.v(lambda v, full=full: v.memset(
                        w2h[:, full:full + 1, :], 0.0), waits=[w2read])
                    wtags.append(kb.ldg(
                        lambda g, rows0=rows0, full=full, rem=rem: g.dma_start(
                            out=w2h[0:rem, full:full + 1, :]
                            .rearrange("p a b -> p (a b)"),
                            in_=w2_w[l].ap()[rows0 + full * P:
                                             rows0 + full * P + rem, :]),
                        waits=[zz]))
                w2_tag = kb.dg.at()

                wguard = None
                geg_last = None
                aguard = None
                for xi, ht in enumerate(hts):
                    wdt = min(P, HID - ht * P)
                    wla = kb.ldg(lambda g, ht=ht, wdt=wdt: g.dma_start(
                        out=wa_sb[:, :, 0:wdt],
                        in_=w1_w[l].ap()[:, ht * P:ht * P + wdt]
                        .rearrange("(a p) c -> p a c", p=P)), waits=[wguard])
                    wlg = kb.ldg(lambda g, ht=ht, wdt=wdt: g.dma_start(
                        out=wg_sb[:, :, 0:wdt],
                        in_=w1_w[l].ap()[:, HID + ht * P:HID + ht * P + wdt]
                        .rearrange("(a p) c -> p a c", p=P)))
                    if wdt < P:
                        kb.v(lambda v: v.memset(wa_sb[:, :, 42:P], 0.0),
                             waits=[wla])
                        wla = kb.v(lambda v: v.memset(wg_sb[:, :, 42:P], 0.0),
                                   waits=[wlg])
                        wlg = wla
                    for c in range(4):
                        cs = slice(c * 512, (c + 1) * 512)
                        for k in range(ND):
                            w = [wla, hn_tag, aguard] if k == 0 else ()
                            kb.t(lambda t, k=k, cs=cs: t.matmul(
                                ps_ag[:, 0:512], wa_sb[:, k, :], hnT[:, k, cs],
                                start=(k == 0), stop=(k == ND - 1)),
                                waits=w, mark=(k == ND - 1))
                        atag = kb.tt.at()
                        for k in range(ND):
                            w = [wlg, aguard] if k == 0 else ()
                            kb.t(lambda t, k=k, cs=cs: t.matmul(
                                ps_ag[:, 512:1024], wg_sb[:, k, :],
                                hnT[:, k, cs],
                                start=(k == 0), stop=(k == ND - 1)),
                                waits=w, mark=(k == ND - 1))
                        gtag = kb.tt.at()
                        gel = fA[4][:, (c % 2) * 512:(c % 2 + 1) * 512]
                        gel_tag = kb.s(lambda s, gel=gel, ht=ht: s.activation(
                            gel, ps_ag[:, 512:1024], AF.Gelu,
                            bias=b1g_c[:, ht:ht + 1]),
                            waits=[gtag, b1_tag, kb.vv.at()])
                        aguard = kb.v(lambda v, xi=xi, cs=cs, gel=gel, ht=ht:
                                      v.scalar_tensor_tensor(
                            geg[:, xi, cs], ps_ag[:, 0:512],
                            b1a_c[:, ht:ht + 1], gel, ALU.add, ALU.mult),
                            waits=[gel_tag, atag])
                    wguard = kb.tt.at()
                    geg_last = aguard

                g2 = [None, None]
                for i in range(NT):
                    pq = i % 2
                    wp = ps_w2[:, pq * D:(pq + 1) * D]
                    for xi in range(nloc):
                        for n in range(2):
                            w = ([w2_tag, geg_last, g2[pq]]
                                 if (xi == 0 and n == 0) else ())
                            laststop = (xi == nloc - 1 and half == 0)
                            kb.t(lambda t, i=i, xi=xi, n=n, wp=wp,
                                 laststop=laststop: t.matmul(
                                wp[:, n * 512:(n + 1) * 512],
                                geg[:, xi, i * P:(i + 1) * P],
                                w2h[:, xi, n * 512:(n + 1) * 512],
                                start=(xi == 0), stop=laststop),
                                waits=w,
                                mark=(xi == nloc - 1 and n == 1 and half == 0))
                    if half == 1:
                        for n in range(2):
                            kb.t(lambda t, n=n, wp=wp: t.matmul(
                                wp[:, n * 512:(n + 1) * 512], onB[0:1, :],
                                b2_sb[0:1, n * 512:(n + 1) * 512],
                                start=False, stop=True),
                                waits=[b2_tag] if n == 0 else (),
                                mark=(n == 1))
                    mtag = kb.tt.at()
                    g2[pq] = kb.v(lambda v, i=i, wp=wp: v.tensor_add(
                        x_res[:, i, :], x_res[:, i, :], wp[:, :]),
                        waits=[mtag])
                w2read = kb.tt.at()
                kb.barrier()
        hn_es.close()
        kb.tap_xres(nc, f"xl{l}", x_res, [kb.vv.at()])

    for l in range(2):
        emit_layer(l, x_resT)

    xload = {}
    ow = [None, None]
    for i in range(2):
        xload[i] = kb.ldy(lambda y, i=i: y.dma_start(
            out=fA[i % 2][:, :], in_=x1_d.ap()[i * P:(i + 1) * P, :]))
    for i in range(NT):
        pq = i % 2
        stag = ln_stats([(lambda i=i: x_res[:, i, :], lambda: fA[4][:, :])], D)
        kb.v(lambda v, i=i: v.tensor_scalar(
            fA[5][:, :], x_res[:, i, :], col("mean"), col("rstd"),
            ALU.subtract, ALU.mult), waits=[stag])
        otag = kb.v(lambda v, pq=pq: v.tensor_add(
            fA[2 + pq][:, :], fA[pq][:, :], fA[5][:, :]),
            waits=[xload[i], ow[pq]])
        if i + 2 < NT:
            xload[i + 2] = kb.ldy(lambda y, i=i, pq=pq: y.dma_start(
                out=fA[pq][:, :], in_=x1_d.ap()[(i + 2) * P:(i + 3) * P, :]),
                waits=[otag])
        ow[pq] = kb.ldy(lambda y, i=i, pq=pq: y.dma_start(
            out=y_out.ap()[i * P:(i + 1) * P, :], in_=fA[2 + pq][:, :]),
            waits=[otag])
    final_dy = kb.dy.at()

    if 0 <= KCUT < len(kb.cuts):
        cut = kb.cuts[KCUT]
        for e in "tvsgy":
            kb.ops[e] = kb.ops[e][:cut[e]]
        print(f"[kernel] truncated at barrier {KCUT}/{len(kb.cuts)}")
    with nc.Block() as block:
        block.tensor(lambda eng: [f(eng) for f in kb.ops["t"]])
        block.vector(lambda eng: [f(eng) for f in kb.ops["v"]])
        block.scalar(lambda eng: [f(eng) for f in kb.ops["s"]])
        block.gpsimd(lambda eng: [f(eng) for f in kb.ops["g"]])

        def sync_body(eng):
            for f in kb.ops["y"]:
                f(eng)
            if KCUT < 0:
                eng.wait_ge(final_dy[0], final_dy[1])
        block.sync(sync_body)

    est.__exit__(None, None, None)
    return nc


def _tables():
    n, dh, win = 4096, 64, 512
    half = np.arange(0, dh, 2, dtype=np.float64)
    inv = 1.0 / 10000.0 ** (half / dh)
    t = np.arange(n, dtype=np.float64)
    fr = t[:, None] * inv[None, :]
    fr = np.concatenate([fr, fr], axis=1)
    base = (half + 0.4 * dh) / (1.4 * dh)
    power = (t - n // 2) / float(win // 2)
    scale = base[None, :] ** power[:, None]
    scale = np.concatenate([scale, scale], axis=1)
    cos, sin = np.cos(fr), np.sin(fr)
    sign = np.where(np.arange(dh) < 32, -1.0, 1.0)[None, :]
    csq = (cos * scale * 8.0).astype(np.float32)
    ssq = (sin * scale * 8.0 * sign).astype(np.float32)
    csk = (cos / scale).astype(np.float32)
    ssk = (sin / scale * sign).astype(np.float32)
    return csq, ssq, csk, ssk


def _masks(odd):
    ii = np.arange(128)[:, None]
    r = np.arange(640)[None, :]
    band = np.where((r >= ii) & (r <= ii + 512), 0.0, NEG).astype(np.float32)
    m0 = np.broadcast_to(band, (4, 128, 640)).copy()
    if not odd:
        for it in range(4):
            jw = it * 128 + np.arange(640)
            m0[it][:, jw < 512] = NEG
    return band, m0.astype(np.float32)


_CACHE = {}


def kernel(**inputs):
    import ml_dtypes
    if "nc" not in _CACHE:
        _CACHE["nc"] = build_nc()
    nc = _CACHE["nc"]

    x = np.asarray(inputs["L0_attn_x"], dtype=np.float32)
    csq, ssq, csk, ssk = _tables()
    rr = np.arange(128)[:, None]
    cc_ = np.arange(128)[None, :]
    com = {
        "wdT": np.ascontiguousarray(np.asarray(inputs["L0_attn_W_delta"]).T),
        "wsT": np.ascontiguousarray(np.asarray(inputs["L0_attn_W_select"]).T),
        "wiT": np.ascontiguousarray(np.asarray(inputs["L0_attn_W_in"]).T),
        "wgT": np.ascontiguousarray(np.asarray(inputs["L0_attn_W_gate"]).T),
        "woT": np.ascontiguousarray(np.asarray(inputs["L0_attn_W_out"]).T),
        "la": np.ascontiguousarray(
            np.asarray(inputs["L0_attn_log_a"]).reshape(8, 128).T),
        "identF": np.eye(128, dtype=np.float32),
        "identB": np.eye(128, dtype=ml_dtypes.bfloat16),
        "onesB": np.ones((128, 128), dtype=ml_dtypes.bfloat16),
        "onesF": np.ones((128, 128), dtype=np.float32),
        "triNL": np.where(cc_ > rr, NEG, 0.0).astype(np.float32),
        "triNU": np.where(cc_ < rr, NEG, 0.0).astype(np.float32),
    }
    for l in range(2):
        com[f"wqkv{l}"] = np.ascontiguousarray(np.asarray(inputs["L0_ffn_wqkv"])[l])
        com[f"wo{l}"] = np.ascontiguousarray(np.asarray(inputs["L0_ffn_wo"])[l])
        com[f"w1_{l}"] = np.ascontiguousarray(np.asarray(inputs["L0_ffn_w1"])[l])
        com[f"w2_{l}"] = np.ascontiguousarray(np.asarray(inputs["L0_ffn_w2"])[l])
        com[f"qs{l}"] = np.broadcast_to(
            np.asarray(inputs["L0_ffn_q_scale"])[l][None, :], (128, 64)).copy()
        com[f"ks{l}"] = np.broadcast_to(
            np.asarray(inputs["L0_ffn_k_scale"])[l][None, :], (128, 64)).copy()
        com[f"lng{l}"] = np.ascontiguousarray(
            np.asarray(inputs["L0_ffn_ln_g"])[l].reshape(8, 128).T)
        com[f"lnb{l}"] = np.ascontiguousarray(
            np.asarray(inputs["L0_ffn_ln_b"])[l].reshape(8, 128).T)
        b1l = np.asarray(inputs["L0_ffn_b1"])[l]
        b1a = np.zeros((128, 22), dtype=np.float32)
        b1g = np.zeros((128, 22), dtype=np.float32)
        for j in range(22):
            w = min(128, 2730 - j * 128)
            b1a[0:w, j] = b1l[j * 128:j * 128 + w]
            b1g[0:w, j] = b1l[2730 + j * 128:2730 + j * 128 + w]
        com[f"b1a{l}"] = b1a
        com[f"b1g{l}"] = b1g
        com[f"b2_{l}"] = np.asarray(inputs["L0_ffn_b2"])[l].reshape(1, -1)
    com = {k: np.ascontiguousarray(np.asarray(v, dtype=v.dtype))
           for k, v in com.items()}

    in_maps = []
    for c in range(8):
        b, hh = c // 2, c % 2
        odd = hh == 1
        mstd, m0 = _masks(odd)
        im = dict(com)
        im["x_in"] = np.ascontiguousarray(x[b, hh * T:(hh + 1) * T, :])
        sl = slice(hh * T, (hh + 1) * T)
        im["csq"] = np.ascontiguousarray(csq[sl])
        im["ssq"] = np.ascontiguousarray(ssq[sl])
        im["csk"] = np.ascontiguousarray(csk[sl])
        im["ssk"] = np.ascontiguousarray(ssk[sl])
        im["mask_std"] = mstd
        im["mask_w0"] = m0
        im["cmask"] = np.full((128, 1), 1.0 if odd else 0.0, dtype=np.float32)
        im["hbias"] = np.full((128, 1), 0.0 if odd else NEG, dtype=np.float32)
        in_maps.append(im)

    res = run_bass_kernel_spmd(nc, in_maps, core_ids=list(range(8)))
    out = np.zeros((4, 4096, 1024), dtype=np.float32)
    for c in range(8):
        b, hh = c // 2, c % 2
        out[b, hh * T:(hh + 1) * T, :] = res.results[c]["y_out"]
    return out


# version 09 (final)
# speedup vs baseline: 1.0141x; 1.0141x over previous
import os
import numpy as np
from contextlib import ExitStack

KCUT = int(os.environ.get("KCUT", "-1"))
KTAPS = [t for t in os.environ.get("KTAPS", "").split(",") if t]
KNOLN = os.environ.get("KNOLN", "") == "1"

import concourse.bass as bass
import concourse.mybir as mybir
from concourse.bass_utils import run_bass_kernel_spmd

F32 = mybir.dt.float32
BF16 = mybir.dt.bfloat16
AF = mybir.ActivationFunctionType
ALU = mybir.AluOpType
AX = mybir.AxisListType

P = 128
T = 2048
NT = 16
D = 1024
ND = 8
H = 8
DH = 64
INNER = 512
NW = 4
HID = 2730
NHT = 22
HALVES = [list(range(0, 11)), list(range(11, 22))]
NEG = -1.0e9
PAIRS = [[0, 1], [2, 3], [4, 5], [6, 7]]


class Ch:
    def __init__(self, nc, es, name):
        self.sem = es.enter_context(nc.semaphore(name))
        self.n = 0

    def inc(self, k=1):
        self.n += k
        return (self.sem, k)

    def at(self):
        return (self.sem, self.n)


class KB:
    def __init__(self, nc, es):
        self.nc = nc
        self.ops = {"t": [], "v": [], "s": [], "g": [], "y": []}
        self.dyp = [Ch(nc, es, f"dy{i}") for i in range(3)]
        self.dgp = [Ch(nc, es, f"dg{i}") for i in range(3)]
        self.dyi = 0
        self.dgi = 0
        self.tt = Ch(nc, es, "tt")
        self.vv = Ch(nc, es, "vv")
        self.ss = Ch(nc, es, "ss")
        self.gg = Ch(nc, es, "gg")
        self.cc = Ch(nc, es, "cc")
        self.bar = es.enter_context(nc.semaphore("bar"))
        self.barn = 0
        self.cuts = []

    def op(self, e, f, waits=(), incs=()):
        flat = []
        for w in waits:
            if w is None:
                continue
            if isinstance(w, list):
                flat.extend(x for x in w if x is not None)
            else:
                flat.append(w)
        waits = tuple(flat)
        incs = tuple(incs)

        def run(eng, f=f, waits=waits, incs=incs):
            for sem, v in waits:
                eng.wait_ge(sem, v)
            inst = f(eng)
            for sem, d in incs:
                inst = inst.then_inc(sem, d)
        self.ops[e].append(run)

    def t(self, f, waits=(), mark=False):
        self.op("t", f, waits, incs=[self.tt.inc()] if mark else ())
        return self.tt.at()

    def v(self, f, waits=()):
        self.op("v", f, waits, incs=[self.vv.inc()])
        return self.vv.at()

    def s(self, f, waits=()):
        self.op("s", f, waits, incs=[self.ss.inc()])
        return self.ss.at()

    def g(self, f, waits=()):
        self.op("g", f, waits, incs=[self.gg.inc()])
        return self.gg.at()

    def ldy(self, f, waits=()):
        ch = self.dyp[self.dyi]
        self.dyi = (self.dyi + 1) % len(self.dyp)
        pre = ch.at()
        self.op("y", f, tuple(waits) + (pre,), incs=[ch.inc(16)])
        return ch.at()

    def ldg(self, f, waits=()):
        ch = self.dgp[self.dgi]
        self.dgi = (self.dgi + 1) % len(self.dgp)
        pre = ch.at()
        self.op("g", f, tuple(waits) + (pre,), incs=[ch.inc(16)])
        return ch.at()

    def dyall(self):
        return [ch.at() for ch in self.dyp]

    def dgall(self):
        return [ch.at() for ch in self.dgp]

    def tap_raw(self, nc, name, ap_fn, shape, dtype, waits):
        if name not in KTAPS:
            return
        h = nc.dram_tensor("tap_" + name, shape, dtype, kind="ExternalOutput")
        self.ldy(lambda y: y.dma_start(out=h.ap(), in_=ap_fn()), waits=waits)

    def tap_xres(self, nc, name, x_res, waits):
        if name not in KTAPS:
            return
        h = nc.dram_tensor("tap_" + name, [T, D], F32, kind="ExternalOutput")
        self.ldy(lambda y: y.dma_start(
            out=h.ap().rearrange("(a p) d -> p a d", p=P),
            in_=x_res[:, :, :]), waits=waits)

    def barrier(self):
        self.barn += 5
        tgt = self.barn
        for e in "tvsgy":
            def run(eng, tgt=tgt):
                eng.sem_inc(self.bar, 1)
                eng.wait_ge(self.bar, tgt)
            self.ops[e].append(run)
        self.cuts.append({e: len(self.ops[e]) for e in "tvsgy"})


def build_nc():
    nc = bass.Bass("TRN2", target_bir_lowering=False)
    nc.detect_race_conditions = False
    est = ExitStack()
    est.__enter__()
    E = est.enter_context

    def din(name, shape, dtype=F32):
        return nc.dram_tensor(name, shape, dtype, kind="ExternalInput")

    x_in = din("x_in", [T, D])
    wdT = din("wdT", [D, D]); wsT = din("wsT", [D, D]); wiT = din("wiT", [D, D])
    wgT = din("wgT", [D, D]); woT = din("woT", [D, D])
    la = din("la", [P, ND])
    wqkv = [din(f"wqkv{l}", [D, 1536]) for l in range(2)]
    wo_w = [din(f"wo{l}", [INNER, D]) for l in range(2)]
    w1_w = [din(f"w1_{l}", [D, 2 * HID]) for l in range(2)]
    w2_w = [din(f"w2_{l}", [HID, D]) for l in range(2)]
    qs_w = [din(f"qs{l}", [P, DH]) for l in range(2)]
    ks_w = [din(f"ks{l}", [P, DH]) for l in range(2)]
    lng_w = [din(f"lng{l}", [P, ND]) for l in range(2)]
    lnb_w = [din(f"lnb{l}", [P, ND]) for l in range(2)]
    b1a_w = [din(f"b1a{l}", [P, NHT]) for l in range(2)]
    b1g_w = [din(f"b1g{l}", [P, NHT]) for l in range(2)]
    b2_w = [din(f"b2_{l}", [1, D]) for l in range(2)]
    csq = din("csq", [T, DH]); ssq = din("ssq", [T, DH])
    csk = din("csk", [T, DH]); ssk = din("ssk", [T, DH])
    identF = din("identF", [P, P]); identB = din("identB", [P, P], BF16)
    onesB = din("onesB", [P, P], BF16); onesF = din("onesF", [P, P])
    triNL = din("triNL", [P, P]); triNU = din("triNU", [P, P])
    mask_std = din("mask_std", [P, 640])
    mask_w0 = din("mask_w0", [4, P, 640])
    cmask = din("cmask", [P, 1])
    hbias = din("hbias", [P, 1])

    y_out = nc.dram_tensor("y_out", [T, D], F32, kind="ExternalOutput")

    x1_d = nc.dram_tensor("x1_d", [T, D], F32)
    gP_d = nc.dram_tensor("gP_d", [ND, P, T], BF16)
    ag1i = nc.dram_tensor("ag1i", [ND, P], F32)
    ag1o = nc.dram_tensor("ag1o", [2 * ND, P], F32)
    ag2i = [nc.dram_tensor(f"ag2i{l}", [1, D], F32) for l in range(2)]
    ag2o = [nc.dram_tensor(f"ag2o{l}", [2, D], F32) for l in range(2)]
    ag4i = [nc.dram_tensor(f"ag4i{l}", [1, D], F32) for l in range(2)]
    ag4o = [nc.dram_tensor(f"ag4o{l}", [2, D], F32) for l in range(2)]
    kvi = [nc.dram_tensor(f"kvi{l}", [P, 4096], BF16) for l in range(2)]
    kvo = [nc.dram_tensor(f"kvo{l}", [2 * P, 4096], BF16) for l in range(2)]

    kb = KB(nc, est)

    x_res = E(nc.sbuf_tensor("x_res", [P, NT, D], F32))
    idF = E(nc.sbuf_tensor("idF", [P, P], F32))
    idB = E(nc.sbuf_tensor("idB", [P, P], BF16))
    onB = E(nc.sbuf_tensor("onB", [P, P], BF16))
    onF = E(nc.sbuf_tensor("onF", [P, P], F32))
    trL = E(nc.sbuf_tensor("trL", [P, P], F32))
    trU = E(nc.sbuf_tensor("trU", [P, P], F32))
    a_col = E(nc.sbuf_tensor("a_col", [P, ND], F32))
    carry = E(nc.sbuf_tensor("carry", [P, ND], F32))
    fin = E(nc.sbuf_tensor("fin", [P, ND], F32))
    cm_col = E(nc.sbuf_tensor("cm_col", [P, 1], F32))
    hb_col = E(nc.sbuf_tensor("hb_col", [P, 1], F32))
    la_sb = E(nc.sbuf_tensor("la_sb", [P, ND], F32))
    lng_c = E(nc.sbuf_tensor("lng_c", [P, ND], F32))
    lnb_c = E(nc.sbuf_tensor("lnb_c", [P, ND], F32))
    b1a_c = E(nc.sbuf_tensor("b1a_c", [P, NHT], F32))
    b1g_c = E(nc.sbuf_tensor("b1g_c", [P, NHT], F32))
    qs_b = E(nc.sbuf_tensor("qs_b", [P, DH], BF16))
    ks_b = E(nc.sbuf_tensor("ks_b", [P, DH], BF16))
    den_s = E(nc.sbuf_tensor("den_s", [1, INNER], F32))
    rd_s = E(nc.sbuf_tensor("rd_s", [1, INNER], F32))
    nmrow = E(nc.sbuf_tensor("nmrow", [1, INNER], BF16))
    cols = E(nc.sbuf_tensor("cols", [P, 32], F32))
    cst = E(nc.sbuf_tensor("cst", [P, 4], F32))
    fA = [E(nc.sbuf_tensor(f"fA{i}", [P, D], F32)) for i in range(6)]
    bA = [E(nc.sbuf_tensor(f"bA{i}", [P, D], BF16)) for i in range(4)]

    CN = ["sum", "sq", "mean", "var", "rstd", "m2", "sum2", "sq2"]

    def col(name):
        i = CN.index(name)
        return cols[:, i:i + 1]

    QN2 = cols[:, 8:16]
    RSTD8 = cols[:, 16:24]
    TMP0 = cols[:, 24:25]

    def ralloc(name, shape, dtype):
        return nc.sbuf_tensor(name, shape, dtype, side="right")

    xrt_stacks = {0: ExitStack(), 1: ExitStack()}

    for dst, src in [(idF, identF), (idB, identB), (onB, onesB), (onF, onesF),
                     (trL, triNL), (trU, triNU), (hb_col, hbias),
                     (la_sb, la), (cm_col, cmask)]:
        kb.ldy(lambda y, d=dst, s=src: y.dma_start(out=d[:, :], in_=s.ap()))
    wl = kb.dyall()

    kb.v(lambda v: v.memset(cst[:, 0:1], 0.5))
    kb.v(lambda v: v.memset(cst[:, 1:2], 1e-24))
    kb.v(lambda v: v.memset(cst[:, 2:3], -1.0))
    cst_tag = kb.v(lambda v: v.memset(cst[:, 3:4], 1e-5))
    C05 = cst[:, 0:1]; CEM24 = cst[:, 1:2]; CM1 = cst[:, 2:3]; CEPS = cst[:, 3:4]

    def ln_stats(srcs, nelem, extra_waits=(), eps=1e-5):
        first = True
        rtags = []
        for ap_fn, scr_fn in srcs:
            rtags.append(kb.v(lambda v, f=ap_fn, first=first: v.tensor_reduce(
                col("sum") if first else col("sum2"), f(), AX.X, ALU.add),
                waits=extra_waits))
            kb.s(lambda s, f=ap_fn, g=scr_fn, first=first: s.activation(
                g(), f(), AF.Square,
                accum_out=col("sq") if first else col("sq2")),
                waits=tuple(extra_waits) + (kb.vv.at(),))
            first = False
        if len(srcs) > 1:
            sum_tag = kb.v(lambda v: v.tensor_add(
                col("sum"), col("sum"), col("sum2")), waits=[rtags[-1]])
            sq_tag = kb.v(lambda v: v.tensor_add(
                col("sq"), col("sq"), col("sq2")), waits=[kb.ss.at()])
        else:
            sum_tag = rtags[0]
            sq_tag = kb.ss.at()
        vh = kb.v(lambda v: v.tensor_scalar(col("var"), col("sq"), 1.0 / nelem,
                                            0.0, ALU.mult, ALU.add),
                  waits=[sq_tag])
        mn = kb.v(lambda v: v.tensor_scalar(col("mean"), col("sum"),
                                            1.0 / nelem, 0.0, ALU.mult,
                                            ALU.add), waits=[sum_tag])
        nv = kb.v(lambda v: v.scalar_tensor_tensor(
            col("var"), col("mean"), col("mean"), col("var"),
            ALU.mult, ALU.subtract), waits=[mn, vh])
        sq_t = kb.s(lambda s: s.activation(col("rstd"), col("var"), AF.Sqrt,
                                           bias=CEPS, scale=-1.0),
                    waits=[nv, cst_tag])
        return kb.v(lambda v: v.reciprocal(col("rstd"), col("rstd")),
                    waits=[sq_t])

    with ExitStack() as es_ssm:
        S = es_ssm.enter_context
        ps_tr = S(nc.psum_tensor("ps_tr", [P, 2 * 512], F32))
        og = S(nc.sbuf_tensor("og", [P, ND, T], BF16))
        ones5 = S(nc.sbuf_tensor("ones5", [P, 512], F32))
        xt = [fA[0], fA[1]]
        gPb1 = S(nc.sbuf_tensor("gPb1", [P, T], BF16))
        gPb = [gPb1, gPb1]

        cm_tag = wl
        st = kb.s(lambda s: s.activation(a_col[:, :], la_sb[:, :], AF.Tanh,
                                         scale=C05), waits=[wl, cst_tag])
        acol_tag = kb.v(lambda v: v.tensor_scalar(a_col[:, :], a_col[:, :],
                                                  0.5, 0.5, ALU.mult, ALU.add),
                        waits=[st])
        kb.v(lambda v: v.memset(ones5[:, :], 1.0))

        with ExitStack() as es12:
            S2 = es12.enter_context
            ps_z = S2(nc.psum_tensor("ps_z", [P, 4 * 512], F32))
            xnT = S2(nc.sbuf_tensor("xnT", [P, ND, T], BF16))
            zw = [[S2(nc.sbuf_tensor(f"zw{pq}_{i}", [P, ND, P], BF16))
                   for i in range(4)] for pq in range(2)]

            wsrc = [wdT, wsT, wiT, wgT]
            wtag = {}
            for m in range(2):
                for i in range(4):
                    wtag[(m, i)] = kb.ldg(lambda g, i=i, m=m: g.dma_start(
                        out=zw[m % 2][i][:, :, :],
                        in_=wsrc[i].ap()[:, m * P:(m + 1) * P]
                        .rearrange("(a p) c -> p a c", p=P)))

            xload = {}
            for i in range(2):
                xload[i] = kb.ldy(lambda y, i=i: y.dma_start(
                    out=xt[i % 2][:, :], in_=x_in.ap()[i * P:(i + 1) * P, :]))
            trg = [None, None]
            fa5_free = None
            for i in range(NT):
                pq = i % 2
                src = xt[pq]
                stag = ln_stats([(lambda src=src: src[:, :],
                                  lambda: fA[4][:, :])], D,
                                extra_waits=[xload[i]])
                xn_tag = kb.v(lambda v, src=src: v.tensor_scalar(
                    fA[5][:, :], src[:, :], col("mean"), col("rstd"),
                    ALU.subtract, ALU.mult), waits=[stag, fa5_free])
                if i + 2 < NT:
                    xload[i + 2] = kb.ldy(lambda y, i=i: y.dma_start(
                        out=xt[i % 2][:, :],
                        in_=x_in.ap()[(i + 2) * P:(i + 3) * P, :]),
                        waits=[xn_tag])
                for k in range(ND):
                    half = k // 4
                    slot = k % 4
                    kb.t(lambda t, half=half, slot=slot, k=k: t.transpose(
                        ps_tr[:, half * 512 + slot * P:half * 512 + (slot + 1) * P],
                        fA[5][:, k * P:(k + 1) * P], idF[:, :]),
                        waits=[xn_tag, trg[half], wl],
                        mark=(k % 4 == 3))
                    if k % 4 == 3:
                        tw = kb.tt.at()
                        ks0 = k - 3
                        trg[half] = kb.v(lambda v, i=i, ks0=ks0, half=half: v.tensor_copy(
                            xnT[:, ks0:ks0 + 4, i * P:(i + 1) * P],
                            ps_tr[:, half * 512:(half + 1) * 512]
                            .rearrange("p (a b) -> p a b", b=P)), waits=[tw])
                fa5_free = kb.tt.at()

            kb.tap_raw(nc, "xnT", lambda: xnT[:, :, :], [P, ND, T], BF16,
                       [kb.vv.at()])
            kb.barrier()

            zguard = None
            gp_guard = [None, None]
            for m in range(ND):
                wpar = m % 2
                wt = [wtag[(m, i)] for i in range(4)]
                for c in range(4):
                    cs = slice(c * 512, (c + 1) * 512)
                    ztags = []
                    for zi in range(4):
                        for k in range(ND):
                            w = [wt[zi], zguard] if k == 0 else ()
                            kb.t(lambda t, zi=zi, k=k, wpar=wpar, cs=cs: t.matmul(
                                ps_z[:, zi * 512:(zi + 1) * 512],
                                zw[wpar][zi][:, k, :], xnT[:, k, cs],
                                start=(k == 0), stop=(k == ND - 1)),
                                waits=w, mark=(k == ND - 1))
                        ztags.append(kb.tt.at())
                    ed = fA[0][:, 0:512]; sp = fA[0][:, 512:1024]
                    ths = fA[1][:, 0:512]; thi = fA[1][:, 512:1024]
                    dec = fA[2][:, 0:512]
                    upd = fA[3][:, 0:512]
                    sc = c % 2
                    stt_ = fA[4][:, sc * 512:(sc + 1) * 512]
                    Pt = fA[5][:, sc * 512:(sc + 1) * 512]
                    stt_prev = fA[4][:, (1 - sc) * 512 + 511:(1 - sc) * 512 + 512]
                    Pt_prev = fA[5][:, (1 - sc) * 512 + 511:(1 - sc) * 512 + 512]
                    kb.s(lambda s, ed=ed: s.activation(ed, ps_z[:, 0:512],
                                                       AF.Exp),
                         waits=[ztags[0], kb.vv.at()])
                    sp_tag = kb.s(lambda s, ed=ed, sp=sp: s.activation(
                        sp, ed, AF.Ln, bias=1.0))
                    kb.s(lambda s, ths=ths: s.activation(
                        ths, ps_z[:, 512:1024], AF.Tanh, scale=C05),
                        waits=[ztags[1]])
                    th_tag = kb.s(lambda s, thi=thi: s.activation(
                        thi, ps_z[:, 1024:1536], AF.Tanh), waits=[ztags[2]])
                    sv = kb.v(lambda v, sp=sp, m=m: v.tensor_scalar(
                        sp, sp, 1e-4, a_col[:, m:m + 1], ALU.max, ALU.mult),
                        waits=[sp_tag, acol_tag])
                    dec_tag = kb.s(lambda s, dec=dec, sp=sp: s.activation(
                        dec, sp, AF.Exp, scale=CM1), waits=[sv])
                    kb.v(lambda v, upd=upd, ths=ths, thi=thi: v.scalar_tensor_tensor(
                        upd, ths, 1.0, thi, ALU.add, ALU.mult), waits=[th_tag])
                    init_s = 0.0 if c == 0 else stt_prev
                    kb.v(lambda v, stt_=stt_, dec=dec, upd=upd, init_s=init_s:
                         v.tensor_tensor_scan(stt_, dec, upd, init_s,
                                              ALU.mult, ALU.add),
                         waits=[dec_tag])
                    init_p = 1.0 if c == 0 else Pt_prev
                    kb.v(lambda v, Pt=Pt, dec=dec, init_p=init_p:
                         v.tensor_tensor_scan(Pt, dec, ones5[:, :], init_p,
                                              ALU.mult, ALU.mult))
                    thg = fA[0][:, 0:512]
                    thg_tag = kb.s(lambda s, thg=thg: s.activation(
                        thg, ps_z[:, 1536:2048], AF.Tanh, scale=C05),
                        waits=[ztags[3]])
                    zguard = kb.ss.at()
                    kb.v(lambda v, m=m, cs=cs, thg=thg, stt_=stt_:
                         v.scalar_tensor_tensor(og[:, m, cs], thg, 1.0, stt_,
                                                ALU.add, ALU.mult),
                         waits=[thg_tag])
                    kb.v(lambda v, cs=cs, thg=thg, Pt=Pt, wpar=wpar:
                         v.scalar_tensor_tensor(gPb[wpar][:, cs], thg, 1.0, Pt,
                                                ALU.add, ALU.mult),
                         waits=[gp_guard[wpar]])
                    if c == 3:
                        kb.v(lambda v, m=m, stt_=stt_: v.tensor_copy(
                            fin[:, m:m + 1], stt_[:, 511:512]))
                gw = kb.ldy(lambda y, m=m: y.dma_start(
                    out=gP_d.ap()[m, :, :], in_=gPb[m % 2][:, :]),
                    waits=[kb.vv.at()])
                gp_guard[0] = gw
                gp_guard[1] = gw
                if m + 2 < ND:
                    for i in range(4):
                        wtag[(m + 2, i)] = kb.ldg(lambda g, i=i, m=m: g.dma_start(
                            out=zw[m % 2][i][:, :, :],
                            in_=wsrc[i].ap()[:, (m + 2) * P:(m + 3) * P]
                            .rearrange("(a p) c -> p a c", p=P)),
                            waits=[kb.tt.at()])

            fin_tag = kb.vv.at()
            kb.tap_raw(nc, "og0", lambda: og[:, :, :], [P, ND, T], BF16,
                       [fin_tag])
            kb.tap_raw(nc, "fin", lambda: fin[:, :], [P, ND], F32, [fin_tag])
            kb.tap_raw(nc, "acol", lambda: a_col[:, :], [P, ND], F32, [fin_tag])
            t1 = kb.t(lambda t: t.transpose(ps_tr[0:ND, 0:P], fin[:, :],
                                            idF[:, :]),
                      waits=[fin_tag], mark=True)
            c1 = kb.v(lambda v: v.tensor_copy(fA[0][0:ND, 0:P],
                                              ps_tr[0:ND, 0:P]), waits=[t1])
            d1 = kb.ldg(lambda g: g.dma_start(out=ag1i.ap(),
                                              in_=fA[0][0:ND, 0:P]), waits=[c1])
            kb.op("g", lambda g: g.collective_compute(
                "AllGather", ALU.bypass, replica_groups=PAIRS,
                ins=[ag1i.ap().opt()], outs=[ag1o.ap().opt()]),
                waits=[d1], incs=[kb.cc.inc()])
            d2 = kb.ldg(lambda g: g.dma_start(out=fA[0][0:ND, 0:P],
                                              in_=ag1o.ap()[0:ND, :]),
                        waits=[kb.cc.at()])
            t2 = kb.t(lambda t: t.transpose(ps_tr[:, 0:ND], fA[0][0:ND, 0:P],
                                            idF[0:ND, 0:ND]), waits=[d2], mark=True)
            car_tag = kb.v(lambda v: v.tensor_scalar(
                carry[:, :], ps_tr[:, 0:ND], cm_col[:, :], 0.0, ALU.mult, ALU.add),
                waits=[t2, cm_tag])
            rb = kb.ldy(lambda y: y.dma_start(
                out=gPb[0][:, :], in_=gP_d.ap()[0, :, :]),
                waits=[gp_guard[0]])
            for m in range(ND):
                fx = kb.v(lambda v, m=m: v.scalar_tensor_tensor(
                    og[:, m, :], gPb[0][:, :], carry[:, m:m + 1],
                    og[:, m, :], ALU.mult, ALU.add),
                    waits=[rb, car_tag])
                if m + 1 < ND:
                    rb = kb.ldy(lambda y, m=m: y.dma_start(
                        out=gPb[0][:, :], in_=gP_d.ap()[m + 1, :, :]),
                        waits=[fx])
            kb.tap_raw(nc, "og1", lambda: og[:, :, :], [P, ND, T], BF16,
                       [kb.vv.at()])
            kb.tap_raw(nc, "carry", lambda: carry[:, :], [P, ND], F32,
                       [kb.vv.at()])
            kb.barrier()

        x_resT = xrt_stacks[0].enter_context(ralloc("x_resT0", [P, ND, T + 1], BF16))
        with ExitStack() as es3:
            S3 = es3.enter_context
            ps_y = S3(nc.psum_tensor("ps_y", [P, 2 * D], F32))
            woutT = S3(nc.sbuf_tensor("woutT", [P, ND, D], BF16))
            wout_tag = kb.ldg(lambda g: g.dma_start(
                out=woutT[:, :, :],
                in_=woT.ap().rearrange("(a p) c -> p a c", p=P)))
            wsc = kb.v(lambda v: v.tensor_scalar(
                woutT[:, :, :].rearrange("p a b -> p (a b)"),
                woutT[:, :, :].rearrange("p a b -> p (a b)"),
                0.25, 0.0, ALU.mult, ALU.add), waits=[wout_tag])
            x1b = [fA[2], fA[3]]
            xload = {}
            for i in range(2):
                xload[i] = kb.ldy(lambda y, i=i: y.dma_start(
                    out=xt[i % 2][:, :], in_=x_in.ap()[i * P:(i + 1) * P, :]))
            yguard = [None, None]
            x1w = [None, None]
            x1guard = [None, None]
            trg = [None, None]
            for i in range(NT):
                pq = i % 2
                yp = ps_y[:, pq * D:(pq + 1) * D]
                for k in range(ND):
                    for n in range(2):
                        w = [yguard[pq], wsc] if (k == 0 and n == 0) else ()
                        kb.t(lambda t, i=i, k=k, n=n, yp=yp: t.matmul(
                            yp[:, n * 512:(n + 1) * 512],
                            og[:, k, i * P:(i + 1) * P],
                            woutT[:, k, n * 512:(n + 1) * 512],
                            start=(k == 0), stop=(k == ND - 1)),
                            waits=w, mark=(k == ND - 1 and n == 1))
                ytag = kb.tt.at()
                stag = ln_stats([(lambda yp=yp: yp[:, :],
                                  lambda: fA[4][:, :])], D,
                                extra_waits=[ytag])
                t1tag = kb.v(lambda v, yp=yp: v.tensor_scalar(
                    fA[5][:, :], yp[:, :], col("mean"), col("rstd"),
                    ALU.subtract, ALU.mult), waits=[stag])
                yguard[pq] = t1tag
                x1tag = kb.v(lambda v, pq=pq: v.tensor_add(
                    x1b[pq][:, :], xt[pq][:, :], fA[5][:, :]),
                    waits=[xload[i], x1w[pq], x1guard[pq]])
                if i + 2 < NT:
                    xload[i + 2] = kb.ldy(lambda y, i=i: y.dma_start(
                        out=xt[i % 2][:, :],
                        in_=x_in.ap()[(i + 2) * P:(i + 3) * P, :]),
                        waits=[x1tag])
                x1w[pq] = kb.ldy(lambda y, i=i, pq=pq: y.dma_start(
                    out=x1_d.ap()[i * P:(i + 1) * P, :], in_=x1b[pq][:, :]),
                    waits=[x1tag])
                stag = ln_stats([(lambda pq=pq: x1b[pq][:, :],
                                  lambda: fA[4][:, :])], D)
                xf_tag = kb.v(lambda v, i=i, pq=pq: v.tensor_scalar(
                    x_res[:, i, :], x1b[pq][:, :], col("mean"), col("rstd"),
                    ALU.subtract, ALU.mult), waits=[stag])
                x1guard[pq] = xf_tag
                for k in range(ND):
                    half = k // 4
                    slot = k % 4
                    kb.t(lambda t, i=i, k=k, half=half, slot=slot: t.transpose(
                        ps_tr[:, half * 512 + slot * P:half * 512 + (slot + 1) * P],
                        x_res[:, i, k * P:(k + 1) * P], idF[:, :]),
                        waits=[xf_tag, trg[half]], mark=(k % 4 == 3))
                    if k % 4 == 3:
                        tw = kb.tt.at()
                        ks0 = k - 3
                        trg[half] = kb.v(lambda v, i=i, ks0=ks0, half=half,
                                         xr=x_resT: v.tensor_copy(
                            xr[:, ks0:ks0 + 4, 1 + i * P:1 + (i + 1) * P],
                            ps_tr[:, half * 512:(half + 1) * 512]
                            .rearrange("p (a b) -> p a b", b=P)), waits=[tw])

    kb.barrier()

    kb.tap_xres(nc, "xf", x_res, [kb.vv.at()])

    def ag_row(l, agi, ago, src_tag):
        d1 = kb.ldg(lambda g, agi=agi: g.dma_start(
            out=agi.ap(), in_=x_res[P - 1:P, NT - 1, :]), waits=[src_tag])
        kb.op("g", lambda g, agi=agi, ago=ago: g.collective_compute(
            "AllGather", ALU.bypass, replica_groups=PAIRS,
            ins=[agi.ap().opt()], outs=[ago.ap().opt()]),
            waits=[d1], incs=[kb.cc.inc()])
        return kb.cc.at()

    def emit_layer(l, x_resT):
        if l == 1:
            x_resT = xrt_stacks[1].enter_context(ralloc("x_resT1", [P, ND, T + 1], BF16))
            with ExitStack() as esr:
                ps_rb = esr.enter_context(nc.psum_tensor("ps_rb", [P, 2 * 512], F32))
                trg = [None, None]
                for i in range(NT):
                    for k in range(ND):
                        half = k // 4
                        slot = k % 4
                        kb.t(lambda t, i=i, k=k, half=half, slot=slot: t.transpose(
                            ps_rb[:, half * 512 + slot * P:
                                  half * 512 + (slot + 1) * P],
                            x_res[:, i, k * P:(k + 1) * P], idF[:, :]),
                            waits=[trg[half]], mark=(k % 4 == 3))
                        if k % 4 == 3:
                            tw = kb.tt.at()
                            ks0 = k - 3
                            trg[half] = kb.v(lambda v, i=i, ks0=ks0, half=half:
                                             v.tensor_copy(
                                x_resT[:, ks0:ks0 + 4, 1 + i * P:1 + (i + 1) * P],
                                ps_rb[:, half * 512:(half + 1) * 512]
                                .rearrange("p (a b) -> p a b", b=P)),
                                waits=[tw])
                kb.barrier()

        with ExitStack() as esl:
            SL = esl.enter_context
            qT = SL(nc.sbuf_tensor(f"qT{l}", [P, 4, T], BF16))
            kT = SL(nc.sbuf_tensor(f"kT{l}", [P, 4, T], BF16))
            v_sb = SL(nc.sbuf_tensor(f"v_sb{l}", [P, NT, INNER], BF16))
            effc = SL(nc.sbuf_tensor(f"effc{l}", [P, NT, DH], BF16))
            effs = SL(nc.sbuf_tensor(f"effs{l}", [P, NT, DH], BF16))
            khalo = SL(nc.sbuf_tensor(f"khalo{l}", [P, 4, 512], BF16))
            vhalo = SL(nc.sbuf_tensor(f"vhalo{l}", [P, 4, 512], BF16))

            with ExitStack() as esq:
                SQ = esq.enter_context
                wsl = SQ(nc.sbuf_tensor(f"wsl{l}", [P, ND, 512], BF16))
                ps_qk = SQ(nc.psum_tensor(f"ps_qk{l}", [P, 2 * 512], F32))
                ps_t2 = SQ(nc.psum_tensor(f"ps_t2{l}", [P, 2 * 512], F32))
                ps_tb = [SQ(nc.psum_tensor(f"ps_tb{l}_{i}", [P, 512], BF16))
                         for i in range(2)]

                qsb_tag = kb.ldg(lambda g: g.dma_start(out=qs_b[:, :],
                                                       in_=qs_w[l].ap()))
                ksb_tag = kb.ldg(lambda g: g.dma_start(out=ks_b[:, :],
                                                       in_=ks_w[l].ap()))
                for dst, srcw in [(lng_c, lng_w[l]), (lnb_c, lnb_w[l]),
                                  (b1a_c, b1a_w[l]), (b1g_c, b1g_w[l])]:
                    kb.ldy(lambda y, d=dst, s=srcw: y.dma_start(out=d[:, :],
                                                                in_=s.ap()))
                lnb_tag = kb.dyall()
                b1_tag = lnb_tag

                hw = ag_row(l, ag2i[l], ag2o[l], kb.vv.at())
                hd = kb.ldg(lambda g: g.dma_start(out=fA[0][0:1, :],
                                                  in_=ag2o[l].ap()[0:1, :]),
                            waits=[hw])
                for k in range(4, 8):
                    kb.t(lambda t, k=k: t.transpose(
                        ps_t2[:, (k - 4):(k - 3)],
                        fA[0][0:1, k * P:(k + 1) * P], idF[0:1, 0:1]),
                        waits=[hd] if k == 4 else (), mark=(k == 7))
                tw = kb.tt.at()
                kb.v(lambda v: v.tensor_scalar(
                    x_resT[:, 4:8, 0:1],
                    ps_t2[:, 0:4].rearrange("p (a b) -> p a b", b=1),
                    cm_col[:, :], 0.0, ALU.mult, ALU.add),
                    waits=[tw, cm_tag])
                xrt_tag = kb.v(lambda v: v.memset(x_resT[:, 0:4, 0:1], 0.0))

                kb.tap_raw(nc, f"xrt{l}", lambda: x_resT[:, :, :],
                           [P, ND, T + 1], BF16, [xrt_tag])
                kb.barrier()

                def build_eff(src_c, src_s, sc_b, after):
                    t1 = kb.ldg(lambda g, src_c=src_c: g.dma_start(
                        out=effc[:, :, :],
                        in_=src_c.ap().rearrange("(a p) b -> p a b", p=P)),
                        waits=[after])
                    t2 = kb.ldg(lambda g, src_s=src_s: g.dma_start(
                        out=effs[:, :, :],
                        in_=src_s.ap().rearrange("(a p) b -> p a b", p=P)),
                        waits=[after])
                    kb.v(lambda v, sc_b=sc_b: v.tensor_mul(
                        effc[:, :, :], effc[:, :, :],
                        sc_b[:, :].unsqueeze(1).broadcast_to([P, NT, DH])),
                        waits=[t1])
                    kb.v(lambda v, sc_b=sc_b: v.tensor_mul(
                        effs[:, :, 0:32], effs[:, :, 0:32],
                        sc_b[:, 32:64].unsqueeze(1).broadcast_to([P, NT, 32])),
                        waits=[t2])
                    return kb.v(lambda v, sc_b=sc_b: v.tensor_mul(
                        effs[:, :, 32:64], effs[:, :, 32:64],
                        sc_b[:, 0:32].unsqueeze(1).broadcast_to([P, NT, 32])))

                def qkv_pass(which, wcol, outT, eff_tag):
                    wload = kb.ldg(lambda g: g.dma_start(
                        out=wsl[:, :, :],
                        in_=wqkv[l].ap()[:, wcol:wcol + 512]
                        .rearrange("(a p) c -> p a c", p=P)),
                        waits=[kb.tt.at()])
                    kb.tap_raw(nc, f"wsl_{which}{l}", lambda: wsl[:, :, :],
                               [P, ND, 512], BF16, [wload])
                    pguard = [None, None]
                    trg2 = [None, None]
                    qb_free = [None, None]
                    ab_free = [None, None]
                    last = None
                    for i in range(NT):
                        pq = i % 2
                        pp = ps_qk[:, pq * 512:(pq + 1) * 512]
                        for k in range(ND):
                            off = 1 + i * P - (1 if k >= 4 else 0)
                            w = [wload, xrt_tag, pguard[pq]] if k == 0 else ()
                            kb.t(lambda t, k=k, off=off, pp=pp: t.matmul(
                                pp[:, :], x_resT[:, k, off:off + P],
                                wsl[:, k, :], start=(k == 0),
                                stop=(k == ND - 1)),
                                waits=w, mark=(k == ND - 1))
                        mtag = kb.tt.at()
                        if which == "v":
                            pguard[pq] = kb.v(lambda v, i=i, pp=pp: v.tensor_copy(
                                v_sb[:, i, :], pp[:, :]), waits=[mtag])
                            last = pguard[pq]
                            continue
                        qb = bA[0][:, pq * 512:(pq + 1) * 512]
                        pguard[pq] = kb.v(lambda v, pp=pp, qb=qb: v.tensor_copy(
                            qb, pp[:, :]), waits=[mtag, qb_free[pq]])
                        sq = fA[4][:, pq * 512:(pq + 1) * 512]
                        kb.v(lambda v, qb=qb, sq=sq: v.tensor_mul(sq, qb, qb))
                        qn2_tag = kb.v(lambda v, sq=sq: v.tensor_reduce(
                            QN2, sq.rearrange("p (a b) -> p a b", b=DH),
                            AX.X, ALU.add), waits=[kb.gg.at()])
                        sqr_tag = kb.s(lambda s: s.activation(
                            RSTD8, QN2, AF.Sqrt, bias=CEM24),
                            waits=[qn2_tag, cst_tag])
                        rstd_tag = kb.v(lambda v: v.reciprocal(RSTD8, RSTD8),
                                        waits=[sqr_tag])
                        qn = bA[1][:, pq * 512:(pq + 1) * 512]
                        qn_tag = kb.g(lambda g, qb=qb, qn=qn: g.tensor_tensor(
                            qn.rearrange("p (a b) -> p a b", b=DH),
                            qb.rearrange("p (a b) -> p a b", b=DH),
                            RSTD8.unsqueeze(2).broadcast_to([P, 8, DH]),
                            ALU.mult), waits=[rstd_tag])
                        qb_free[pq] = qn_tag
                        at = bA[2][:, pq * 512:(pq + 1) * 512]
                        bt = bA[3][:, pq * 512:(pq + 1) * 512]
                        kb.v(lambda v, qn=qn, at=at, i=i: v.tensor_mul(
                            at.rearrange("p (a b) -> p a b", b=DH),
                            qn.rearrange("p (a b) -> p a b", b=DH),
                            effc[:, i, :].unsqueeze(1).broadcast_to([P, 8, DH])),
                            waits=[qn_tag, eff_tag, ab_free[pq]])
                        kb.v(lambda v, qn=qn, bt=bt, i=i: v.tensor_mul(
                            bt.rearrange("p (a b) -> p a b", b=DH)[:, :, 0:32],
                            qn.rearrange("p (a b) -> p a b", b=DH)[:, :, 32:64],
                            effs[:, i, 0:32].unsqueeze(1).broadcast_to([P, 8, 32])))
                        kb.v(lambda v, qn=qn, bt=bt, i=i: v.tensor_mul(
                            bt.rearrange("p (a b) -> p a b", b=DH)[:, :, 32:64],
                            qn.rearrange("p (a b) -> p a b", b=DH)[:, :, 0:32],
                            effs[:, i, 32:64].unsqueeze(1).broadcast_to([P, 8, 32])))
                        qp_tag = kb.v(lambda v, at=at, bt=bt: v.tensor_add(
                            at, at, bt))
                        for h2 in range(4):
                            half = h2 % 2
                            tw = kb.t(lambda t, at=at, h2=h2, half=half:
                                      t.transpose(
                                ps_tb[half][:, 0:P],
                                at[:, h2 * P:(h2 + 1) * P], idB[:, :]),
                                waits=[qp_tag, trg2[half]], mark=True)
                            trg2[half] = kb.v(lambda v, i=i, h2=h2, half=half,
                                              outT=outT: v.tensor_copy(
                                outT[:, h2, i * P:(i + 1) * P],
                                ps_tb[half][:, 0:P]),
                                waits=[tw])
                        ab_free[pq] = kb.tt.at()
                        last = trg2[0]
                    return kb.vv.at()

                eq = build_eff(csq, ssq, qs_b, qsb_tag)
                q_done = qkv_pass("q", 0, qT, eq)
                ek = build_eff(csk, ssk, ks_b, q_done)
                k_done = qkv_pass("k", 512, kT, ek)
                v_done = qkv_pass("v", 1024, None, None)
                kb.tap_raw(nc, f"q{l}", lambda: qT[:, :, :], [P, 4, T], BF16,
                           [q_done])
                kb.tap_raw(nc, f"k{l}", lambda: kT[:, :, :], [P, 4, T], BF16,
                           [k_done])
                kb.tap_raw(nc, f"v{l}", lambda: v_sb[:, :, :],
                           [P, NT, INNER], BF16, [v_done])

                d1 = kb.ldg(lambda g: g.dma_start(
                    out=kvi[l].ap()[:, 0:2048],
                    in_=kT[:, :, 1536:2048]),
                    waits=[k_done])
                d1b = kb.ldg(lambda g: g.dma_start(
                    out=kvi[l].ap()[:, 2048:4096],
                    in_=v_sb[:, 12:16, :]),
                    waits=[v_done])
                kb.op("g", lambda g: g.collective_compute(
                    "AllGather", ALU.bypass, replica_groups=PAIRS,
                    ins=[kvi[l].ap().opt()], outs=[kvo[l].ap().opt()]),
                    waits=[d1, d1b], incs=[kb.cc.inc()])
                kh_tag = kb.ldg(lambda g: g.dma_start(
                    out=khalo[:, :, :].rearrange("p a b -> p (a b)"),
                    in_=kvo[l].ap()[0:P, 0:2048]), waits=[kb.cc.at()])
                halo_kv = kb.ldg(lambda g: g.dma_start(
                    out=vhalo[:, :, :].rearrange("p a b -> p (a b)"),
                    in_=kvo[l].ap()[0:P, 2048:4096]))
                kb.tap_raw(nc, f"kh{l}", lambda: khalo[:, :, :],
                           [P, 4, 512], BF16, [kh_tag])
                kb.tap_raw(nc, f"vh{l}", lambda: vhalo[:, :, :],
                           [P, 4, 512], BF16, [halo_kv])
                kb.barrier()
            xrt_stacks[l].close()

            oT_es = ExitStack()
            oT = oT_es.enter_context(ralloc(f"oT{l}", [P, 4, T], BF16))
            with ExitStack() as esa:
                SA = esa.enter_context
                eT = SA(nc.sbuf_tensor(f"eT{l}", [P, 8, 512], BF16))
                rdb_sb = SA(nc.sbuf_tensor(f"rdb{l}", [P, 512], F32))
                msk = SA(nc.sbuf_tensor(f"msk{l}", [P, 640], F32))
                msk0 = SA(nc.sbuf_tensor(f"msk0{l}", [P, 4, 640], F32))
                ps_mx = SA(nc.psum_tensor(f"ps_mx{l}", [P, 2 * D], F32))
                ps_sT = SA(nc.psum_tensor(f"ps_sT{l}", [P, 2 * 512], F32))
                ps_av = SA(nc.psum_tensor(f"ps_av{l}", [P, 512], F32))
                ps_dn = SA(nc.psum_tensor(f"ps_dn{l}", [P, 512], F32))

                kb.ldy(lambda y: y.dma_start(out=msk[:, :], in_=mask_std.ap()))
                m2 = kb.ldy(lambda y: y.dma_start(
                    out=msk0[:, :, :], in_=mask_w0.ap().transpose([1, 0, 2])))
                ez = kb.v(lambda v: v.memset(
                    eT[:, :, :].rearrange("p a b -> p (a b)"), 0.0))

                mx_guard = [None, None]
                sT_guard = [None, None]
                av_guard = None
                dn_guard = None
                eT_guard = None
                for w in range(NW):
                    for h in range(H):
                        hr = (h % 2) * DH
                        ht = h // 2
                        cpos = (0, hr)
                        for it in range(4):
                            mq = it % 2
                            mp = ps_mx[:, mq * D:(mq + 1) * D]
                            for jh in range(2):
                                def mmA(t, w=w, jh=jh, hr=hr, ht=ht, it=it, mp=mp):
                                    if w == 0 and jh == 0:
                                        rhs = khalo[hr:hr + DH, ht, 0:512]
                                    else:
                                        base = (w - 1 + jh) * 512
                                        rhs = kT[hr:hr + DH, ht, base:base + 512]
                                    return t.matmul(
                                        mp[:, jh * 512:(jh + 1) * 512],
                                        qT[hr:hr + DH, ht,
                                           w * 512 + it * P:w * 512 + (it + 1) * P],
                                        rhs, start=True, stop=True)
                                kb.t(mmA,
                                     waits=[mx_guard[mq], halo_kv] if jh == 0 else (),
                                     mark=(jh == 1))
                            mtag = kb.tt.at()
                            def ttr(v, w=w, it=it, mp=mp):
                                mask = msk0[:, it, :] if w == 0 else msk[:, :]
                                return v.tensor_add(
                                    fA[4][:, 0:640],
                                    mp[:, it * P:it * P + 640], mask)
                            mx_guard[mq] = kb.v(ttr, waits=[mtag, m2, kb.tt.at()])
                            red_tag = kb.v(lambda v: v.tensor_reduce(
                                TMP0, fA[4][:, 0:640], AX.X, ALU.max))
                            tw = kb.t(lambda t: t.transpose(
                                ps_dn[0:1, 0:P], TMP0, idF[:, :]),
                                waits=[red_tag, dn_guard], mark=True)
                            dn_guard = kb.v(lambda v, it=it: v.tensor_scalar(
                                nmrow[0:1, it * P:(it + 1) * P],
                                ps_dn[0:1, 0:P], -1.0, 0.0, ALU.mult, ALU.add),
                                waits=[tw])
                        nm_tag = dn_guard

                        jt_tags = []
                        for jt in range(8):
                            sq_ = jt % 2
                            sp_ = ps_sT[:, sq_ * 512:(sq_ + 1) * 512]
                            o = (jt % 4) * P
                            lo, hi = (0, o + P) if jt < 4 else (o, 512)
                            def mmB(t, w=w, jt=jt, hr=hr, ht=ht, sp_=sp_,
                                    lo=lo, hi=hi):
                                if w == 0 and jt < 4:
                                    lhs = khalo[hr:hr + DH, ht, jt * P:(jt + 1) * P]
                                elif jt < 4:
                                    base = (w - 1) * 512
                                    lhs = kT[hr:hr + DH, ht,
                                             base + jt * P:base + (jt + 1) * P]
                                else:
                                    base = w * 512
                                    lhs = kT[hr:hr + DH, ht,
                                             base + (jt - 4) * P:base + (jt - 3) * P]
                                return t.matmul(
                                    sp_[:, lo:hi], lhs,
                                    qT[hr:hr + DH, ht, w * 512 + lo:w * 512 + hi],
                                    start=True, stop=False)
                            kb.op("t", mmB, waits=[sT_guard[sq_]])
                            kb.t(lambda t, sp_=sp_, lo=lo, hi=hi: t.matmul(
                                sp_[:, lo:hi], onB[0:1, :], nmrow[0:1, lo:hi],
                                start=False, stop=True),
                                waits=[nm_tag], mark=True)
                            smt = kb.tt.at()
                            tri_tag = kb.v(lambda v, jt=jt, sp_=sp_, o=o:
                                           v.tensor_add(
                                sp_[:, o:o + P], sp_[:, o:o + P],
                                (trL if jt < 4 else trU)[:, :]), waits=[smt])
                            def ex(s, w=w, jt=jt, sp_=sp_, lo=lo, hi=hi):
                                bias = hb_col[:, :] if (w == 0 and jt < 4) else 0.0
                                return s.activation(eT[:, jt, lo:hi],
                                                    sp_[:, lo:hi], AF.Exp,
                                                    bias=bias)
                            sT_guard[sq_] = kb.s(ex, waits=[tri_tag, eT_guard, ez])
                            jt_tags.append(kb.ss.at())
                        for jt in range(8):
                            kb.t(lambda t, jt=jt: t.matmul(
                                ps_dn[0:1, 0:512], onB[:, 0:1], eT[:, jt, :],
                                start=(jt == 0), stop=(jt == 7)),
                                waits=[jt_tags[jt]] + ([dn_guard] if jt == 0 else []),
                                mark=(jt == 7))
                        den_tag = kb.tt.at()
                        for jt in range(8):
                            def amm(t, w=w, jt=jt, h=h, hr=hr, cpos=cpos):
                                if w == 0 and jt < 4:
                                    lhs = vhalo[:, jt, h * DH:(h + 1) * DH]
                                else:
                                    ti = (w - 1) * 4 + jt
                                    lhs = v_sb[:, ti, h * DH:(h + 1) * DH]
                                return t.matmul(ps_av[hr:hr + DH, :], lhs,
                                                eT[:, jt, :],
                                                start=(jt == 0), stop=(jt == 7),
                                                tile_position=cpos)
                            kb.t(amm, waits=[av_guard] if jt == 0 else (),
                                 mark=(jt == 7))
                        av_tag = kb.tt.at()
                        eT_guard = av_tag
                        dn_guard = kb.v(lambda v: v.tensor_copy(
                            den_s[0:1, :], ps_dn[0:1, 0:512]), waits=[den_tag])
                        rd_tag = kb.v(lambda v: v.reciprocal(rd_s[0:1, :],
                                                             den_s[0:1, :]),
                                      waits=[dn_guard])
                        rdb_tag = kb.t(lambda t, hr=hr, cpos=cpos: t.matmul(
                            ps_sT[hr:hr + DH, 0:512], onF[0:1, 0:DH],
                            rd_s[0:1, :], start=True, stop=True,
                            tile_position=cpos),
                            waits=[rd_tag, sT_guard[0]], mark=True)
                        rc = kb.v(lambda v, hr=hr: v.tensor_copy(
                            rdb_sb[hr:hr + DH, :], ps_sT[hr:hr + DH, 0:512]),
                            waits=[rdb_tag])
                        av_guard = kb.v(lambda v, w=w, hr=hr, ht=ht: v.tensor_mul(
                            oT[hr:hr + DH, ht, w * 512:(w + 1) * 512],
                            ps_av[hr:hr + DH, :], rdb_sb[hr:hr + DH, :]),
                            waits=[av_tag])
                        sT_guard[0] = rc
                att_done = kb.vv.at()
                kb.tap_raw(nc, f"o{l}", lambda: oT[:, :, :], [P, 4, T], BF16,
                           [att_done])
            kb.barrier()

        with ExitStack() as esw:
            SW = esw.enter_context
            wo_sb = SW(nc.sbuf_tensor(f"wo_sb{l}", [P, 4, D], BF16))
            ps_wo = SW(nc.psum_tensor(f"ps_wo{l}", [P, 2 * D], F32))
            wol = kb.ldg(lambda g: g.dma_start(
                out=wo_sb[:, :, :],
                in_=wo_w[l].ap().rearrange("(a p) c -> p a c", p=P)))
            guard = [None, None]
            for i in range(NT):
                pq = i % 2
                wp = ps_wo[:, pq * D:(pq + 1) * D]
                for kt in range(4):
                    for n in range(2):
                        w = [wol, att_done, guard[pq]] if (kt == 0 and n == 0) else ()
                        kb.t(lambda t, i=i, kt=kt, n=n, wp=wp: t.matmul(
                            wp[:, n * 512:(n + 1) * 512],
                            oT[:, kt, i * P:(i + 1) * P],
                            wo_sb[:, kt, n * 512:(n + 1) * 512],
                            start=(kt == 0), stop=(kt == 3)),
                            waits=w, mark=(kt == 3 and n == 1))
                mtag = kb.tt.at()
                guard[pq] = kb.v(lambda v, i=i, wp=wp: v.tensor_add(
                    x_res[:, i, :], x_res[:, i, :], wp[:, :]), waits=[mtag])
            xmid_tag = kb.vv.at()
        oT_es.close()
        kb.tap_xres(nc, f"xm{l}", x_res, [xmid_tag])
        kb.barrier()
        halo4 = ag_row(l, ag4i[l], ag4o[l], xmid_tag)
        kb.barrier()

        hn_es = ExitStack()
        hnT = hn_es.enter_context(ralloc(f"hnT{l}", [P, ND, T], BF16))
        with ExitStack() as esn:
            SN = esn.enter_context
            h2h = SN(nc.sbuf_tensor(f"h2h{l}", [P, NT, 512], F32))
            ps_h = SN(nc.psum_tensor(f"ps_h{l}", [P, 2 * 512], F32))
            s1 = kb.ldy(lambda y: y.dma_start(
                out=h2h[1:P, :, :], in_=x_res[0:P - 1, :, 512:1024]))
            s2 = kb.ldy(lambda y: y.dma_start(
                out=h2h[0:1, 1:NT, :], in_=x_res[P - 1:P, 0:NT - 1, 512:1024]))
            s3 = kb.ldg(lambda g: g.dma_start(
                out=h2h[0:1, 0:1, :].rearrange("p a b -> p (a b)"),
                in_=ag4o[l].ap()[0:1, 512:1024]), waits=[halo4])
            sh = kb.v(lambda v: v.tensor_scalar(
                h2h[0:1, 0:1, :].rearrange("p a b -> p (a b)"),
                h2h[0:1, 0:1, :].rearrange("p a b -> p (a b)"),
                cm_col[0:1, :], 0.0, ALU.mult, ALU.add), waits=[s3, s1, s2])
            trg = [None, None]
            fa5_free = None
            for i in range(NT):
                stag = ln_stats(
                    [(lambda i=i: x_res[:, i, 0:512], lambda: fA[4][:, 0:512]),
                     (lambda i=i: h2h[:, i, :], lambda: fA[4][:, 512:1024])],
                    D, extra_waits=[sh])
                kb.v(lambda v, i=i: v.tensor_scalar(
                    fA[5][:, 0:512], x_res[:, i, 0:512], col("mean"),
                    col("rstd"), ALU.subtract, ALU.mult),
                    waits=[stag, fa5_free])
                t1_tag = kb.v(lambda v, i=i: v.tensor_scalar(
                    fA[5][:, 512:1024], h2h[:, i, :], col("mean"),
                    col("rstd"), ALU.subtract, ALU.mult))
                for k in range(ND):
                    half = k // 4
                    tw = kb.t(lambda t, k=k, half=half: t.transpose(
                        ps_h[:, half * 512 + (k % 4) * P:
                             half * 512 + (k % 4) * P + P],
                        fA[5][:, k * P:(k + 1) * P], idF[:, :]),
                        waits=[t1_tag, trg[half]], mark=True)
                    trg[half] = kb.v(lambda v, i=i, k=k, half=half: v.tensor_scalar(
                        hnT[:, k, i * P:(i + 1) * P],
                        ps_h[:, half * 512 + (k % 4) * P:
                             half * 512 + (k % 4) * P + P],
                        lng_c[:, k:k + 1], lnb_c[:, k:k + 1],
                        ALU.mult, ALU.add), waits=[tw, lnb_tag])
                fa5_free = kb.tt.at()
            hn_tag = kb.vv.at()
        kb.barrier()

        with ExitStack() as es8:
            S8 = es8.enter_context
            geg = S8(nc.sbuf_tensor(f"geg{l}", [P, 11, T], BF16))
            b2_sb = bA[0]
            b2_tag = kb.ldg(lambda g: g.dma_start(out=b2_sb[0:1, :],
                                                  in_=b2_w[l].ap()))
            w2h = S8(nc.sbuf_tensor(f"w2h{l}", [P, 11, D], BF16))
            wa_sb = S8(nc.sbuf_tensor(f"wa{l}", [P, ND, P], BF16))
            wg_sb = S8(nc.sbuf_tensor(f"wg{l}", [P, ND, P], BF16))
            ps_ag = S8(nc.psum_tensor(f"ps_ag{l}", [P, 2 * 512], F32))
            ps_w2 = S8(nc.psum_tensor(f"ps_w2{l}", [P, 2 * D], F32))

            w2read = None
            for half in range(2):
                hts = HALVES[half]
                nloc = len(hts)
                rows0 = hts[0] * P
                nrows = min(HID, (hts[-1] + 1) * P) - rows0
                full = nrows // P
                wtags = [kb.ldg(lambda g, rows0=rows0, full=full: g.dma_start(
                    out=w2h[:, 0:full, :],
                    in_=w2_w[l].ap()[rows0:rows0 + full * P, :]
                    .rearrange("(a p) c -> p a c", p=P)),
                    waits=[w2read])]
                if nrows % P:
                    rem = nrows % P
                    zz = kb.v(lambda v, full=full: v.memset(
                        w2h[:, full:full + 1, :], 0.0), waits=[w2read])
                    wtags.append(kb.ldg(
                        lambda g, rows0=rows0, full=full, rem=rem: g.dma_start(
                            out=w2h[0:rem, full:full + 1, :]
                            .rearrange("p a b -> p (a b)"),
                            in_=w2_w[l].ap()[rows0 + full * P:
                                             rows0 + full * P + rem, :]),
                        waits=[zz]))
                w2_tag = list(wtags)

                wguard = None
                geg_last = None
                aguard = None
                for xi, ht in enumerate(hts):
                    wdt = min(P, HID - ht * P)
                    wla = kb.ldg(lambda g, ht=ht, wdt=wdt: g.dma_start(
                        out=wa_sb[:, :, 0:wdt],
                        in_=w1_w[l].ap()[:, ht * P:ht * P + wdt]
                        .rearrange("(a p) c -> p a c", p=P)), waits=[wguard])
                    wlg = kb.ldg(lambda g, ht=ht, wdt=wdt: g.dma_start(
                        out=wg_sb[:, :, 0:wdt],
                        in_=w1_w[l].ap()[:, HID + ht * P:HID + ht * P + wdt]
                        .rearrange("(a p) c -> p a c", p=P)))
                    if wdt < P:
                        kb.v(lambda v: v.memset(wa_sb[:, :, 42:P], 0.0),
                             waits=[wla])
                        wla = kb.v(lambda v: v.memset(wg_sb[:, :, 42:P], 0.0),
                                   waits=[wlg])
                        wlg = wla
                    for c in range(4):
                        cs = slice(c * 512, (c + 1) * 512)
                        for k in range(ND):
                            w = [wla, hn_tag, aguard] if k == 0 else ()
                            kb.t(lambda t, k=k, cs=cs: t.matmul(
                                ps_ag[:, 0:512], wa_sb[:, k, :], hnT[:, k, cs],
                                start=(k == 0), stop=(k == ND - 1)),
                                waits=w, mark=(k == ND - 1))
                        atag = kb.tt.at()
                        for k in range(ND):
                            w = [wlg, aguard] if k == 0 else ()
                            kb.t(lambda t, k=k, cs=cs: t.matmul(
                                ps_ag[:, 512:1024], wg_sb[:, k, :],
                                hnT[:, k, cs],
                                start=(k == 0), stop=(k == ND - 1)),
                                waits=w, mark=(k == ND - 1))
                        gtag = kb.tt.at()
                        gel = fA[4][:, (c % 2) * 512:(c % 2 + 1) * 512]
                        gel_tag = kb.s(lambda s, gel=gel, ht=ht: s.activation(
                            gel, ps_ag[:, 512:1024], AF.Gelu,
                            bias=b1g_c[:, ht:ht + 1]),
                            waits=[gtag, b1_tag, kb.vv.at()])
                        aguard = kb.v(lambda v, xi=xi, cs=cs, gel=gel, ht=ht:
                                      v.scalar_tensor_tensor(
                            geg[:, xi, cs], ps_ag[:, 0:512],
                            b1a_c[:, ht:ht + 1], gel, ALU.add, ALU.mult),
                            waits=[gel_tag, atag])
                    wguard = kb.tt.at()
                    geg_last = aguard

                g2 = [None, None]
                for i in range(NT):
                    pq = i % 2
                    wp = ps_w2[:, pq * D:(pq + 1) * D]
                    for xi in range(nloc):
                        for n in range(2):
                            w = ([w2_tag, geg_last, g2[pq]]
                                 if (xi == 0 and n == 0) else ())
                            laststop = (xi == nloc - 1 and half == 0)
                            kb.t(lambda t, i=i, xi=xi, n=n, wp=wp,
                                 laststop=laststop: t.matmul(
                                wp[:, n * 512:(n + 1) * 512],
                                geg[:, xi, i * P:(i + 1) * P],
                                w2h[:, xi, n * 512:(n + 1) * 512],
                                start=(xi == 0), stop=laststop),
                                waits=w,
                                mark=(xi == nloc - 1 and n == 1 and half == 0))
                    if half == 1:
                        for n in range(2):
                            kb.t(lambda t, n=n, wp=wp: t.matmul(
                                wp[:, n * 512:(n + 1) * 512], onB[0:1, :],
                                b2_sb[0:1, n * 512:(n + 1) * 512],
                                start=False, stop=True),
                                waits=[b2_tag] if n == 0 else (),
                                mark=(n == 1))
                    mtag = kb.tt.at()
                    g2[pq] = kb.v(lambda v, i=i, wp=wp: v.tensor_add(
                        x_res[:, i, :], x_res[:, i, :], wp[:, :]),
                        waits=[mtag])
                w2read = kb.tt.at()
                kb.barrier()
        hn_es.close()
        kb.tap_xres(nc, f"xl{l}", x_res, [kb.vv.at()])

    for l in range(2):
        emit_layer(l, x_resT)

    xload = {}
    ow = [None, None]
    x1_stored = kb.dyall()
    for i in range(2):
        xload[i] = kb.ldy(lambda y, i=i: y.dma_start(
            out=fA[i % 2][:, :], in_=x1_d.ap()[i * P:(i + 1) * P, :]),
            waits=[x1_stored])
    for i in range(NT):
        pq = i % 2
        stag = ln_stats([(lambda i=i: x_res[:, i, :], lambda: fA[4][:, :])], D)
        kb.v(lambda v, i=i: v.tensor_scalar(
            fA[5][:, :], x_res[:, i, :], col("mean"), col("rstd"),
            ALU.subtract, ALU.mult), waits=[stag])
        otag = kb.v(lambda v, pq=pq: v.tensor_add(
            fA[2 + pq][:, :], fA[pq][:, :], fA[5][:, :]),
            waits=[xload[i], ow[pq]])
        if i + 2 < NT:
            xload[i + 2] = kb.ldy(lambda y, i=i, pq=pq: y.dma_start(
                out=fA[pq][:, :], in_=x1_d.ap()[(i + 2) * P:(i + 3) * P, :]),
                waits=[otag])
        ow[pq] = kb.ldy(lambda y, i=i, pq=pq: y.dma_start(
            out=y_out.ap()[i * P:(i + 1) * P, :], in_=fA[2 + pq][:, :]),
            waits=[otag])
    final_dy = kb.dyall()

    if 0 <= KCUT < len(kb.cuts):
        cut = kb.cuts[KCUT]
        for e in "tvsgy":
            kb.ops[e] = kb.ops[e][:cut[e]]
        print(f"[kernel] truncated at barrier {KCUT}/{len(kb.cuts)}")
    with nc.Block() as block:
        block.tensor(lambda eng: [f(eng) for f in kb.ops["t"]])
        block.vector(lambda eng: [f(eng) for f in kb.ops["v"]])
        block.scalar(lambda eng: [f(eng) for f in kb.ops["s"]])
        block.gpsimd(lambda eng: [f(eng) for f in kb.ops["g"]])

        def sync_body(eng):
            for f in kb.ops["y"]:
                f(eng)
            if KCUT < 0:
                for sem, v in final_dy:
                    eng.wait_ge(sem, v)
        block.sync(sync_body)

    est.__exit__(None, None, None)
    return nc


def _tables():
    n, dh, win = 4096, 64, 512
    half = np.arange(0, dh, 2, dtype=np.float64)
    inv = 1.0 / 10000.0 ** (half / dh)
    t = np.arange(n, dtype=np.float64)
    fr = t[:, None] * inv[None, :]
    fr = np.concatenate([fr, fr], axis=1)
    base = (half + 0.4 * dh) / (1.4 * dh)
    power = (t - n // 2) / float(win // 2)
    scale = base[None, :] ** power[:, None]
    scale = np.concatenate([scale, scale], axis=1)
    cos, sin = np.cos(fr), np.sin(fr)
    sign = np.where(np.arange(dh) < 32, -1.0, 1.0)[None, :]
    csq = (cos * scale * 8.0).astype(np.float32)
    ssq = (sin * scale * 8.0 * sign).astype(np.float32)
    csk = (cos / scale).astype(np.float32)
    ssk = (sin / scale * sign).astype(np.float32)
    return csq, ssq, csk, ssk


def _masks(odd):
    ii = np.arange(128)[:, None]
    r = np.arange(640)[None, :]
    band = np.where((r >= ii) & (r <= ii + 512), 0.0, NEG).astype(np.float32)
    m0 = np.broadcast_to(band, (4, 128, 640)).copy()
    if not odd:
        for it in range(4):
            jw = it * 128 + np.arange(640)
            m0[it][:, jw < 512] = NEG
    return band, m0.astype(np.float32)


_CACHE = {}


def kernel(**inputs):
    import ml_dtypes
    if "nc" not in _CACHE:
        _CACHE["nc"] = build_nc()
    nc = _CACHE["nc"]

    x = np.asarray(inputs["L0_attn_x"], dtype=np.float32)
    csq, ssq, csk, ssk = _tables()
    rr = np.arange(128)[:, None]
    cc_ = np.arange(128)[None, :]
    com = {
        "wdT": np.ascontiguousarray(np.asarray(inputs["L0_attn_W_delta"]).T),
        "wsT": np.ascontiguousarray(np.asarray(inputs["L0_attn_W_select"]).T),
        "wiT": np.ascontiguousarray(np.asarray(inputs["L0_attn_W_in"]).T),
        "wgT": np.ascontiguousarray(np.asarray(inputs["L0_attn_W_gate"]).T),
        "woT": np.ascontiguousarray(np.asarray(inputs["L0_attn_W_out"]).T),
        "la": np.ascontiguousarray(
            np.asarray(inputs["L0_attn_log_a"]).reshape(8, 128).T),
        "identF": np.eye(128, dtype=np.float32),
        "identB": np.eye(128, dtype=ml_dtypes.bfloat16),
        "onesB": np.ones((128, 128), dtype=ml_dtypes.bfloat16),
        "onesF": np.ones((128, 128), dtype=np.float32),
        "triNL": np.where(cc_ > rr, NEG, 0.0).astype(np.float32),
        "triNU": np.where(cc_ < rr, NEG, 0.0).astype(np.float32),
    }
    for l in range(2):
        com[f"wqkv{l}"] = np.ascontiguousarray(np.asarray(inputs["L0_ffn_wqkv"])[l])
        com[f"wo{l}"] = np.ascontiguousarray(np.asarray(inputs["L0_ffn_wo"])[l])
        com[f"w1_{l}"] = np.ascontiguousarray(np.asarray(inputs["L0_ffn_w1"])[l])
        com[f"w2_{l}"] = np.ascontiguousarray(np.asarray(inputs["L0_ffn_w2"])[l])
        com[f"qs{l}"] = np.broadcast_to(
            np.asarray(inputs["L0_ffn_q_scale"])[l][None, :], (128, 64)).copy()
        com[f"ks{l}"] = np.broadcast_to(
            np.asarray(inputs["L0_ffn_k_scale"])[l][None, :], (128, 64)).copy()
        com[f"lng{l}"] = np.ascontiguousarray(
            np.asarray(inputs["L0_ffn_ln_g"])[l].reshape(8, 128).T)
        com[f"lnb{l}"] = np.ascontiguousarray(
            np.asarray(inputs["L0_ffn_ln_b"])[l].reshape(8, 128).T)
        b1l = np.asarray(inputs["L0_ffn_b1"])[l]
        b1a = np.zeros((128, 22), dtype=np.float32)
        b1g = np.zeros((128, 22), dtype=np.float32)
        for j in range(22):
            w = min(128, 2730 - j * 128)
            b1a[0:w, j] = b1l[j * 128:j * 128 + w]
            b1g[0:w, j] = b1l[2730 + j * 128:2730 + j * 128 + w]
        com[f"b1a{l}"] = b1a
        com[f"b1g{l}"] = b1g
        com[f"b2_{l}"] = np.asarray(inputs["L0_ffn_b2"])[l].reshape(1, -1)
    com = {k: np.ascontiguousarray(np.asarray(v, dtype=v.dtype))
           for k, v in com.items()}

    in_maps = []
    for c in range(8):
        b, hh = c // 2, c % 2
        odd = hh == 1
        mstd, m0 = _masks(odd)
        im = dict(com)
        im["x_in"] = np.ascontiguousarray(x[b, hh * T:(hh + 1) * T, :])
        sl = slice(hh * T, (hh + 1) * T)
        im["csq"] = np.ascontiguousarray(csq[sl])
        im["ssq"] = np.ascontiguousarray(ssq[sl])
        im["csk"] = np.ascontiguousarray(csk[sl])
        im["ssk"] = np.ascontiguousarray(ssk[sl])
        im["mask_std"] = mstd
        im["mask_w0"] = m0
        im["cmask"] = np.full((128, 1), 1.0 if odd else 0.0, dtype=np.float32)
        im["hbias"] = np.full((128, 1), 0.0 if odd else NEG, dtype=np.float32)
        in_maps.append(im)

    res = run_bass_kernel_spmd(nc, in_maps, core_ids=list(range(8)))
    out = np.zeros((4, 4096, 1024), dtype=np.float32)
    for c in range(8):
        b, hh = c // 2, c % 2
        out[b, hh * T:(hh + 1) * T, :] = res.results[c]["y_out"]
    return out
